# version 03
# baseline (speedup 1.0000x reference)
import math
import os
import numpy as np

import concourse.bass as bass
import concourse.mybir as mybir
from concourse.bass_utils import run_bass_kernel_spmd

F32 = mybir.dt.float32
F32R = mybir.dt.float32r
BF16 = mybir.dt.bfloat16
AF = mybir.ActivationFunctionType
ALU = mybir.AluOpType
AX = mybir.AxisListType

ENGINES = ("te", "act", "dve", "pool", "sp")
DMA_QUEUES = {"qsp": "sp", "qpool": "pool", "qact": "act"}

T_TOK = 2048
NTILE = 16
D = 1024
NCHUNK = 8


def _overlap(a, b):
    if a == "all" or b == "all":
        return True
    ta, tb = isinstance(a, tuple), isinstance(b, tuple)
    if ta and tb:
        return a[0] < b[1] and b[0] < a[1]
    if ta or tb:
        return False
    return a == b


class Task:
    __slots__ = ("engine", "emits", "reads", "writes", "queue", "note", "qclk")

    def __init__(self, engine, emits, reads, writes, queue=None, note=""):
        self.engine = engine
        self.emits = emits
        self.reads = reads
        self.writes = writes
        self.queue = queue
        self.note = note


QSUB = {"qsp": 4, "qpool": 2, "qact": 4}


class Prog:
    def __init__(self, nc):
        self.nc = nc
        self.tasks = {e: [] for e in ENGINES}
        self.state = {}
        self.counts = {e: 0 for e in ENGINES}
        self.qcounts = {q: 0 for q in DMA_QUEUES}
        self.qclocks = [f"{q}#{i}" for q in DMA_QUEUES for i in range(QSUB[q])]
        self.vc = {e: {} for e in ENGINES}
        self.hist = {k: [] for k in list(ENGINES) + self.qclocks}
        self.waits = {e: [] for e in ENGINES}
        self.n_waits = 0
        self.n_tasks = 0
        self.n_instr = 0

    def _deps_for(self, reads, writes):
        deps = []
        for (buf, key) in reads:
            for ent in self.state.get(buf, []):
                if _overlap(key, ent[0]) and ent[1] is not None:
                    deps.append(ent[1])
        for (buf, key) in writes:
            for ent in self.state.get(buf, []):
                if _overlap(key, ent[0]):
                    if ent[1] is not None:
                        deps.append(ent[1])
                    deps.extend(ent[2])
        return deps

    def _update_state(self, task, rec):
        for (buf, key) in task.writes:
            lst = self.state.setdefault(buf, [])
            if key == "all":
                lst.clear()
                lst.append([key, rec, []])
                continue
            found = False
            for ent in lst:
                if ent[0] == key:
                    ent[1] = rec
                    ent[2] = []
                    found = True
            if not found:
                lst.append([key, rec, []])
        for (buf, key) in task.reads:
            lst = self.state.setdefault(buf, [])
            for ent in lst:
                if ent[0] == key:
                    ent[2].append(rec)
                    break
            else:
                lst.append([key, None, [rec]])

    def add(self, engine, emits, reads=(), writes=(), queue=None, note=""):
        if not isinstance(emits, (list, tuple)):
            emits = [emits]
        task = Task(engine, list(emits), list(reads), list(writes), queue, note)
        deps = self._deps_for(task.reads, task.writes)
        if queue is not None:
            total = self.qcounts[queue]
            sub = total % QSUB[queue]
            clk = f"{queue}#{sub}"
            count = total // QSUB[queue] + 1
            if count > 1:
                deps = list(deps) + [(clk, count - 1, None)]
        myvc = self.vc[engine]
        needed = {}
        for (ck, cnt, dvc) in deps:
            if myvc.get(ck, 0) >= cnt:
                continue
            if needed.get(ck, 0) < cnt:
                needed[ck] = cnt
        wl = []
        for ck, cnt in sorted(needed.items(), key=lambda x: str(x[0])):
            wl.append((ck, cnt))
            myvc[ck] = max(myvc.get(ck, 0), cnt)
            pvc = self.hist[ck][cnt - 1]
            for k2, c2 in pvc.items():
                if myvc.get(k2, 0) < c2:
                    myvc[k2] = c2
        self.waits[engine].append(wl)
        self.n_waits += len(wl)
        self.n_tasks += 1
        self.n_instr += len(task.emits)

        if queue is not None:
            self.qcounts[queue] += 1
        else:
            self.counts[engine] += 1
            count = self.counts[engine]
            clk = engine
        task.qclk = clk if queue is not None else None
        snap = dict(myvc)
        self.hist[clk].append(snap)
        self._update_state(task, (clk, count, snap))
        self.tasks[engine].append(task)
        return task

    def render(self):
        nc = self.nc
        sems = {}
        for name in list(ENGINES) + self.qclocks:
            sems[name] = nc.alloc_semaphore(f"sem_{name.replace('#', '_')}")
        self.sems = sems
        qset = set(self.qclocks)

        def run_engine(eng_obj, ename):
            for i, task in enumerate(self.tasks[ename]):
                for (ck, cnt) in self.waits[ename][i]:
                    mult = 16 if ck in qset else 1
                    eng_obj.wait_ge(sems[ck], cnt * mult)
                last = None
                for fn in task.emits:
                    last = fn(eng_obj)
                if task.queue is not None:
                    last.then_inc(sems[task.qclk], 16)
                else:
                    last.then_inc(sems[ename], 1)

        with nc.Block() as block:
            if self.tasks["sp"]:
                block.sync(lambda e: run_engine(e, "sp"))
            if self.tasks["te"]:
                block.tensor(lambda e: run_engine(e, "te"))
            if self.tasks["dve"]:
                block.vector(lambda e: run_engine(e, "dve"))
            if self.tasks["act"]:
                block.scalar(lambda e: run_engine(e, "act"))
            if self.tasks["pool"]:
                block.gpsimd(lambda e: run_engine(e, "pool"))


class Carve:
    __slots__ = ("name", "ap2d", "off", "nbytes", "esz", "n")

    def __init__(self, name, ap2d, off, nbytes, esz):
        self.name = name
        self.ap2d = ap2d
        self.off = off
        self.nbytes = nbytes
        self.esz = esz
        self.n = nbytes // esz

    def __getitem__(self, sl):
        return self.ap2d[sl]

    @property
    def ap(self):
        return self.ap2d

    def k(self, lo=None, hi=None):
        if lo is None:
            return ("POOL", (self.off, self.off + self.nbytes))
        return ("POOL", (self.off + lo * self.esz, self.off + hi * self.esz))


class Env:
    def __init__(self, nc, pool_bytes=None):
        self.nc = nc
        self.prog = Prog(nc)
        self.x = nc.alloc_sbuf_tensor("x_tm", [128, NTILE * D], F32)
        self.bnst = nc.alloc_sbuf_tensor("bnst", [128, NTILE * 12], F32)
        self.mv = nc.alloc_sbuf_tensor("mv", [128, NTILE * 2], F32)
        self.mrs = nc.alloc_sbuf_tensor("mrs", [128, NTILE * 2], F32)
        self.ones_bf = nc.alloc_sbuf_tensor("ones_bf", [1, 128], BF16)
        self.ones_f = nc.alloc_sbuf_tensor("ones_f", [1, 128], F32)
        self.pvec = nc.alloc_sbuf_tensor("pvec", [128, 64], F32)
        self.ones_col = nc.alloc_sbuf_tensor("ones_col", [128, 1], BF16)
        self.pb = [nc.alloc_psum_tensor(f"pb{i}", [128, 512], F32) for i in range(7)]
        self.pbT = nc.alloc_psum_tensor("pbT", [128, 1024], BF16)
        self._rr = 0
        if pool_bytes is None:
            pool_bytes = nc.sbuf_bytes_remaining - 512
            pool_bytes -= pool_bytes % 64
        self.pool = nc.alloc_sbuf_tensor("POOL", [128, pool_bytes], mybir.dt.uint8)
        self.pool_bytes = pool_bytes
        self.pool_bf = self.pool.ap().bitcast(BF16)
        self.pool_f32 = self.pool.ap().bitcast(F32)
        self.carves = {}

    def carve(self, name, off, n_el, dtype):
        esz = 2 if dtype == BF16 else 4
        assert off % 64 == 0, (name, off)
        assert off + n_el * esz <= self.pool_bytes, (name, off, n_el * esz, self.pool_bytes)
        base = self.pool_bf if dtype == BF16 else self.pool_f32
        ap2d = base[:, off // esz: off // esz + n_el]
        c = Carve(name, ap2d, off, n_el * esz, esz)
        self.carves[name] = c
        return c

    def bank(self):
        b = self.pb[self._rr % 6]
        self._rr += 1
        return b

    def dma(self, queue, out_ap, in_ap, reads, writes, note=""):
        eng = DMA_QUEUES[queue]
        reads = list(reads) + [("XPOSE", "all")]
        writes = list(writes)
        if queue == "qpool":
            writes.append(("QPOOL_SER", "all"))
        return self.prog.add(eng, lambda e: e.dma_start(out_ap, in_ap),
                             reads, writes, queue=queue, note=note)

    def dma_slow(self, queue, out_ap, in_ap, reads, writes, note=""):
        eng = DMA_QUEUES[queue]
        reads = list(reads) + [("XPOSE", "all")]
        return self.prog.add(
            eng, lambda e: e.dma_start(out_ap, in_ap, allow_slow_non_contiguous=True),
            reads, writes, queue=queue, note=note)

    def xpose(self, out_ap, in_ap, reads, writes, note="xp"):
        return self.prog.add("sp", lambda e: e.dma_start_transpose(out_ap, in_ap),
                             list(reads), list(writes) + [("XPOSE", "all")],
                             queue="qsp", note=note)

    def tap(self, name, ap, reads):
        if not hasattr(self, "taps"):
            self.taps = []
        t = self.nc.dram_tensor(f"tap_{name}", list(ap.shape), ap.dtype,
                                kind="ExternalOutput").ap()
        self.dma("qsp", t, ap, reads, [("TAPS", name)], note=f"tap_{name}")

    def init_consts(self):
        p = self.prog
        p.add("dve", lambda e: e.memset(self.ones_bf[:], 1.0), [], [("ones_bf", "all")])
        p.add("dve", lambda e: e.memset(self.ones_f[:], 1.0), [], [("ones_f", "all")])
        p.add("dve", lambda e: e.memset(self.ones_col[:], 1.0), [], [("ones_col", "all")])

    def ln_stats(self, src_ap_fn, src_reads, tiles=range(NTILE)):
        p = self.prog
        tiles = list(tiles)
        for t in tiles:
            p.add("dve", [
                (lambda e, t=t: e.bn_stats(self.bnst.ap()[:, t*12:t*12+6], src_ap_fn(t)[:, 0:512])),
                (lambda e, t=t: e.bn_stats(self.bnst.ap()[:, t*12+6:t*12+12], src_ap_fn(t)[:, 512:1024])),
            ], src_reads(t), [("bnst", t)])
            p.add("dve", lambda e, t=t: e.bn_aggr(self.mv.ap()[:, t*2:t*2+2],
                                                  self.bnst.ap()[:, t*12:t*12+12]),
                  [("bnst", t)], [("mv", t)])
        mv3 = self.mv.ap().rearrange("p (t two) -> p t two", two=2)
        mrs3 = self.mrs.ap().rearrange("p (t two) -> p t two", two=2)
        lo, hi = min(tiles), max(tiles) + 1
        var_ap = mv3[:, lo:hi, 1]
        rstd_ap = mrs3[:, lo:hi, 1]
        p.add("dve", lambda e: e.tensor_scalar_add(rstd_ap, var_ap, 1e-5),
              [("mv", t) for t in tiles], [("mrs", t) for t in tiles])
        p.add("act", lambda e: e.activation(rstd_ap, rstd_ap, AF.Sqrt),
              [("mrs", t) for t in tiles], [("mrs", t) for t in tiles])
        p.add("dve", lambda e: e.reciprocal(rstd_ap, rstd_ap),
              [("mrs", t) for t in tiles], [("mrs", t) for t in tiles])

    def ln_apply(self, src_ap_fn, src_reads, dst_ap_fn, dst_writes, tiles=range(NTILE),
                 engine="dve"):
        p = self.prog
        mv3 = self.mv.ap().rearrange("p (t two) -> p t two", two=2)
        mrs3 = self.mrs.ap().rearrange("p (t two) -> p t two", two=2)
        for t in tiles:
            p.add(engine, lambda e, t=t: e.tensor_scalar(
                      dst_ap_fn(t), src_ap_fn(t),
                      mv3[:, t, 0:1], mrs3[:, t, 1:2], ALU.subtract, ALU.mult),
                  list(src_reads(t)) + [("mrs", t), ("mv", t)], dst_writes(t))

    def ln_apply_add(self, s_ap_fn, s_reads, tiles=range(NTILE)):
        p = self.prog
        mv3 = self.mv.ap().rearrange("p (t two) -> p t two", two=2)
        mrs3 = self.mrs.ap().rearrange("p (t two) -> p t two", two=2)
        xap = self.x.ap()
        for t in tiles:
            p.add("dve", lambda e, t=t: e.tensor_scalar(
                      s_ap_fn(t), s_ap_fn(t),
                      mv3[:, t, 0:1], mrs3[:, t, 1:2], ALU.subtract, ALU.mult),
                  list(s_reads(t)) + [("mrs", t), ("mv", t)], list(s_reads(t)))
            p.add("dve", lambda e, t=t: e.tensor_add(
                      xap[:, t*D:(t+1)*D], xap[:, t*D:(t+1)*D], s_ap_fn(t)),
                  [("x_tm", t)] + list(s_reads(t)), [("x_tm", t)])

    def tm2fm(self, tm_carve, fmi_carve, tiles=NTILE, feat=D):
        n = tiles * feat
        out3 = fmi_carve.ap[:, 0:n].rearrange("p (m t) -> p m t", t=128)
        self.xpose(out3, tm_carve.ap[:, 0:n],
                   [tm_carve.k(0, n)], [fmi_carve.k(0, n)])

    def mm(self, psum_ap, mms, reads, writes, note="", start=True, stop=True):
        emits = []
        n = len(mms)
        for i, (l, r) in enumerate(mms):
            emits.append(lambda e, l=l, r=r, s=(start and i == 0),
                         t=(stop and i == n - 1):
                         e.matmul(psum_ap, l, r, start=s, stop=t,
                                  skip_group_check=not (start and stop)))
        return self.prog.add("te", emits, reads, writes, note=note)


def ceil_div(a, b):
    return (a + b - 1) // b



def load_weight_T(env, hbm_w, wT, stage, n_out, k_in, queue="qpool"):
    ntiles = n_out // 128
    src = hbm_w.rearrange("(u p) c -> p u c", p=128)
    env.dma(queue, stage.ap.rearrange("p (u c) -> p u c", u=ntiles), src,
            [], [stage.k()], note="wld")
    env.tm2fm(stage, wT, tiles=ntiles, feat=k_in)


def load_weight_N(env, hbm_w, wN, n_in, m_out, queue="qpool"):
    ktiles = n_in // 128
    src = hbm_w.rearrange("(u p) m -> p u m", p=128)
    env.dma(queue, wN.ap.rearrange("p (u m) -> p u m", u=ktiles), src,
            [], [wN.k()], note="wldN")


def load_vec_pc(env, hbm_v, dst_ap, dst_writes, nchunk=8, queue="qsp"):
    v3 = hbm_v.rearrange("(c p) -> c p", p=128)
    for c in range(nchunk):
        env.dma_slow(queue, dst_ap[:, c:c+1], v3[c][:, None],
                     [], dst_writes, note="vec")


def load_row(env, hbm_v, row_ap, writes, queue="qsp"):
    env.dma(queue, row_ap, hbm_v[None, :], [], writes, note="row")



def slot_L0_ffn(env, w):
    p = env.prog
    KB = 1024
    XLN = env.carve("ffn_xln", 0, 16 * KB, BF16)
    S2 = env.carve("ffn_s2", 32768, 8 * KB, BF16)
    S3 = env.carve("ffn_s3", 49152, 8 * KB, BF16)
    S4 = env.carve("ffn_s4", 65536, 8 * KB, BF16)
    S5 = env.carve("ffn_s5", 81920, 8 * KB, BF16)
    SQ = env.carve("ffn_s", 98304, 2 * KB, F32)
    WST = env.carve("ffn_wst", 106496, 8 * KB, BF16)
    WT = env.carve("ffn_wT", 122880, 8 * KB, BF16)
    MUB = env.carve("ffn_mub", 139264, 2 * KB, BF16)
    MUR = env.carve("ffn_murow", 98304, 2 * KB, F32)

    load_row(env, w["L0_ffn_mu_r"], MUR.ap[0:1, 0:1024], [MUR.k(0, 1024)])
    load_row(env, w["L0_ffn_mu_k"], MUR.ap[0:1, 1024:2048], [MUR.k(1024, 2048)])
    for i in range(4):
        bk = env.bank()
        env.mm(bk.ap()[:, 0:512],
               [(env.ones_f.ap()[0:1, :], MUR.ap[0:1, i*512:(i+1)*512])],
               [("ones_f", "all"), MUR.k(i*512, (i+1)*512)], [(bk.name, "all")])
        p.add("act", lambda e, i=i, bk=bk: e.copy(MUB.ap[:, i*512:(i+1)*512], bk.ap()[:, 0:512]),
              [(bk.name, "all")], [MUB.k(i*512, (i+1)*512)])

    xap = env.x.ap()
    env.ln_stats(lambda t: xap[:, t*D:(t+1)*D], lambda t: [("x_tm", t)])
    env.ln_apply(lambda t: xap[:, t*D:(t+1)*D], lambda t: [("x_tm", t)],
                 lambda t: XLN.ap[:, t*D:(t+1)*D], lambda t: [XLN.k(t*D, (t+1)*D)])

    def fmi_lhsT(cv, kc, j, feat=D):
        o = j * feat + kc * 128
        return cv.ap[:, o:o+128]

    def fmi_rhs(cv, kc, u0, u1, feat=D):
        return cv.ap.rearrange("p (u wq) -> p u wq", wq=feat)[:, u0:u1, kc*128:(kc+1)*128]

    for hf in range(2):
        hb = hf * 8 * D
        p.add("dve", lambda e: e.memset(S2.ap[0:1, 0:D], 0.0), [], [S2.k(0, D)])
        env.dma("qsp", S2.ap[1:128, :], XLN.ap[0:127, hb:hb + 8*D],
                [XLN.k(hb, hb + 8*D)], [S2.k()], note="shiftA")
        env.dma("qsp", S2.ap[0:1, D:8*D], XLN.ap[127:128, hb:hb + 7*D],
                [XLN.k(hb, hb + 7*D)], [S2.k()], note="shiftB")
        if hf == 1:
            env.dma("qsp", S2.ap[0:1, 0:D], XLN.ap[127:128, 7*D:8*D],
                    [XLN.k(7*D, 8*D)], [S2.k(0, D)], note="shiftC")
        for tl in range(8):
            o = tl * D
            xo = hb + o
            for (dst, mi, eng) in ((S3, 0, "dve"), (S4, 1, "pool")):
                dsl = dst.ap[:, o:o+D]
                p.add(eng, lambda e, dsl=dsl, xo=xo, o=o: e.tensor_sub(
                          dsl, XLN.ap[:, xo:xo+D], S2.ap[:, o:o+D]),
                      [XLN.k(xo, xo+D), S2.k(o, o+D)], [dst.k(o, o+D)])
                p.add(eng, lambda e, dsl=dsl, mi=mi: e.tensor_mul(
                          dsl, dsl, MUB.ap[:, mi*1024:(mi+1)*1024]),
                      [dst.k(o, o+D), MUB.k(mi*1024, (mi+1)*1024)], [dst.k(o, o+D)])
                p.add(eng, lambda e, dsl=dsl, o=o: e.tensor_add(
                          dsl, dsl, S2.ap[:, o:o+D]),
                      [dst.k(o, o+D), S2.k(o, o+D)], [dst.k(o, o+D)])
        env.tm2fm(S4, S5, tiles=8, feat=D)
        env.tm2fm(S3, S2, tiles=8, feat=D)
        load_weight_T(env, w["L0_ffn_W_r"], WT, WST, 1024, 1024)
        for jl in range(8):
            for h2 in range(2):
                bk = env.bank()
                mms = [(fmi_lhsT(S2, i, jl), fmi_rhs(WT, i, h2*4, h2*4+4)) for i in range(8)]
                env.mm(bk.ap()[:, 0:512], mms, [S2.k(), WT.k()], [(bk.name, "all")])
                o = jl*1024 + h2*512
                p.add("act", lambda e, o=o, bk=bk: e.activation(
                          S3.ap[:, o:o+512], bk.ap()[:, 0:512], AF.Sigmoid),
                      [(bk.name, "all")], [S3.k(o, o+512)])
        load_weight_T(env, w["L0_ffn_W_k"], WT, WST, 1024, 1024)
        for dd in range(8):
            for g2 in range(2):
                bk = env.bank()
                mms = [(fmi_lhsT(WT, i, dd), fmi_rhs(S5, i, g2*4, g2*4+4)) for i in range(8)]
                env.mm(bk.ap()[:, 0:512], mms, [S5.k(), WT.k()], [(bk.name, "all")])
                dst = S4.ap.rearrange("p (u wq) -> p u wq", wq=D)[:, g2*4:g2*4+4, dd*128:(dd+1)*128]
                kk = [S4.k(g2*4*D, (g2*4+4)*D)]
                p.add("act", lambda e, dst=dst, bk=bk: e.activation(dst, bk.ap()[:, 0:512], AF.Relu),
                      [(bk.name, "all")], kk)
                p.add("act", lambda e, dst=dst: e.activation(dst, dst, AF.Square), kk, kk)
        load_weight_T(env, w["L0_ffn_W_v"], WT, WST, 1024, 1024)
        for qq in range(4):
            q0 = qq * 2
            for jj in range(2):
                jl = q0 + jj
                for h2 in range(2):
                    bk = env.bank()
                    mms = [(fmi_lhsT(S4, i, jl), fmi_rhs(WT, i, h2*4, h2*4+4)) for i in range(8)]
                    env.mm(bk.ap()[:, 0:512], mms, [S4.k(), WT.k()], [(bk.name, "all")])
                    so = jj*1024 + h2*512
                    ro = jl*1024 + h2*512
                    p.add("dve", lambda e, so=so, ro=ro, bk=bk: e.scalar_tensor_tensor(
                              SQ.ap[:, so:so+512], bk.ap()[:, 0:512], 1.0,
                              S3.ap[:, ro:ro+512], ALU.mult, ALU.mult),
                          [(bk.name, "all"), S3.k(ro, ro+512)], [SQ.k(so, so+512)])
            t0 = hf * 8 + q0
            tiles = range(t0, t0 + 2)
            env.ln_stats(lambda t, t0=t0: SQ.ap[:, (t-t0)*D:(t-t0+1)*D],
                         lambda t, t0=t0: [SQ.k((t-t0)*D, (t-t0+1)*D)], tiles=tiles)
            env.ln_apply_add(lambda t, t0=t0: SQ.ap[:, (t-t0)*D:(t-t0+1)*D],
                             lambda t, t0=t0: [SQ.k((t-t0)*D, (t-t0+1)*D)],
                             tiles=tiles)



def slot_L1_attn(env, w):
    p = env.prog
    KB = 1024
    XF = env.carve("l1a_xf", 0, 16 * KB, BF16)
    XLN = env.carve("l1a_xln", 32768, 16 * KB, BF16)
    WA = env.carve("l1a_waiv", 32768, 8 * 3072, BF16)
    CD = env.carve("l1a_cd", 32768, 4 * KB, F32)
    Q5 = env.carve("l1a_q5", 49152, 4 * KB, F32)
    Q6 = env.carve("l1a_q6", 65536, 4 * KB, F32)
    A_ = env.carve("l1a_a", 81920, 4 * KB, F32)
    H_ = env.carve("l1a_h", 81920, 4 * KB, BF16)
    HF = env.carve("l1a_hf", 90112, 4 * KB, BF16)
    I_ = env.carve("l1a_i", 98304, 4 * KB, BF16)
    V_ = env.carve("l1a_v", 106496, 4 * KB, BF16)
    WM = env.carve("l1a_wmix", 98304, 8 * KB, BF16)
    SQ = env.carve("l1a_s", 65536, KB, F32)
    ROWS = env.carve("l1a_rows", 114688, 4 * KB, F32)
    ROWB = env.carve("l1a_rowb", 131072, 2 * KB, BF16)
    CONST = env.carve("l1a_const", 135168, 384, F32)
    CONB = env.carve("l1a_conb", 136704, 256, BF16)
    DBC = env.carve("l1a_dbc", 137216, KB, F32)
    BMC = env.carve("l1a_bmc", 141312, KB, BF16)

    def row(col):
        return ROWS.ap[0:1, col*KB:(col+1)*KB]
    r_ccd, r_cwc, r_cfs, r_ctd = [row(i) for i in range(4)]
    r_incA = ROWB.ap[0:1, 0:1024]
    r_incB = ROWB.ap[0:1, 1024:2048]
    RK = ROWS.k()
    RBK = ROWB.k()

    TRI = CONST.ap[:, 0:128]
    IND1 = CONB.ap[0:1, 0:128]
    IND2 = CONB.ap[0:1, 128:256]
    CK = CONST.k()
    CBK = CONB.k()
    p.add("pool", lambda e: e.iota(TRI, pattern=[[0, 128]], base=0, channel_multiplier=1,
                                   allow_small_or_imprecise_dtypes=True), [], [CK])
    p.add("pool", lambda e: e.iota(Q5.ap[:, 0:128], pattern=[[1, 128]], base=0,
                                   channel_multiplier=0, allow_small_or_imprecise_dtypes=True),
          [], [Q5.k(0, 128)])
    p.add("dve", lambda e: e.tensor_tensor(Q6.ap[:, 0:128], TRI, Q5.ap[:, 0:128], ALU.is_le),
          [CK, Q5.k(0, 128)], [Q6.k(0, 128)])
    p.add("dve", lambda e: e.tensor_single_scalar(Q5.ap[:, 128:256], Q5.ap[:, 0:128], 63.5, ALU.is_gt),
          [Q5.k(0, 128)], [Q5.k(128, 256)])
    p.add("dve", lambda e: e.tensor_single_scalar(Q5.ap[:, 256:384], TRI, 63.5, ALU.is_le),
          [CK], [Q5.k(256, 384)])
    p.add("dve", lambda e: e.tensor_mul(Q5.ap[:, 128:256], Q5.ap[:, 128:256], Q5.ap[:, 256:384]),
          [Q5.k(128, 384)], [Q5.k(128, 256)])
    p.add("dve", lambda e: e.tensor_sub(TRI, Q6.ap[:, 0:128], Q5.ap[:, 128:256]),
          [Q6.k(0, 128), Q5.k(128, 256)], [CK])
    p.add("pool", lambda e: e.iota(IND1, pattern=[[1, 128]], base=0, channel_multiplier=0,
                                   allow_small_or_imprecise_dtypes=True), [], [CBK])
    p.add("pool", lambda e: e.iota(IND2, pattern=[[1, 128]], base=0, channel_multiplier=0,
                                   allow_small_or_imprecise_dtypes=True), [], [CBK])
    p.add("dve", lambda e: e.tensor_single_scalar(IND1, IND1, 63.5, ALU.is_le), [CBK], [CBK])
    p.add("dve", lambda e: e.tensor_single_scalar(IND2, IND2, 63.5, ALU.is_gt), [CBK], [CBK])
    p.add("dve", lambda e: e.memset(r_ccd, 1.0), [], [RK])
    p.add("dve", lambda e: e.memset(r_cwc, 0.0), [], [RK])
    dbr = Q5.ap[0:1, 0:KB]
    load_row(env, w["L1_attn_decay_bias"], dbr, [Q5.k(0, KB)])
    for h2 in range(2):
        bk = env.bank()
        env.mm(bk.ap()[:, 0:512], [(env.ones_f.ap()[0:1, :], dbr[:, h2*512:(h2+1)*512])],
               [("ones_f", "all"), Q5.k(0, KB)], [(bk.name, "all")])
        p.add("act", lambda e, h2=h2, bk=bk: e.copy(DBC.ap[:, h2*512:(h2+1)*512], bk.ap()[:, 0:512]),
              [(bk.name, "all")], [DBC.k(h2*512, (h2+1)*512)])
    bmr = Q5.ap[0:1, KB:2*KB]
    load_row(env, w["L1_attn_b_mix"], bmr, [Q5.k(KB, 2*KB)])
    for h2 in range(2):
        bk = env.bank()
        env.mm(bk.ap()[:, 0:512], [(env.ones_f.ap()[0:1, :], bmr[:, h2*512:(h2+1)*512])],
               [("ones_f", "all"), Q5.k(KB, 2*KB)], [(bk.name, "all")])
        p.add("act", lambda e, h2=h2, bk=bk: e.copy(BMC.ap[:, h2*512:(h2+1)*512], bk.ap()[:, 0:512]),
              [(bk.name, "all")], [BMC.k(h2*512, (h2+1)*512)])

    xap = env.x.ap()
    env.ln_stats(lambda t: xap[:, t*D:(t+1)*D], lambda t: [("x_tm", t)])
    env.ln_apply(lambda t: xap[:, t*D:(t+1)*D], lambda t: [("x_tm", t)],
                 lambda t: XLN.ap[:, t*D:(t+1)*D], lambda t: [XLN.k(t*D, (t+1)*D)])
    env.tm2fm(XLN, XF)

    def fmi_lhsT(cv, kc, j, feat=D):
        o = j * feat + kc * 128
        return cv.ap[:, o:o+128]

    TRIr = TRI

    for q in range(4):
        t0 = q * 4
        load_weight_N(env, w["L1_attn_W_aiv"], WA, 1024, 3072)
        for jl in range(4):
            j = t0 + jl
            for mg in range(6):
                bk = env.bank()
                mms = [(fmi_lhsT(XF, i, j),
                        WA.ap[:, i*3072 + mg*512: i*3072 + (mg+1)*512]) for i in range(8)]
                env.mm(bk.ap()[:, 0:512], mms, [XF.k(), WA.k()], [(bk.name, "all")])
                if mg < 2:
                    sl = slice(jl*KB + mg*512, jl*KB + (mg+1)*512)
                    p.add("dve", lambda e, sl=sl, bk=bk, mg=mg: e.tensor_add(
                              A_.ap[:, sl], bk.ap()[:, 0:512], DBC.ap[:, mg*512:(mg+1)*512]),
                          [(bk.name, "all"), DBC.k(mg*512, (mg+1)*512)], [A_.k(sl.start, sl.stop)])
                    p.add("act", lambda e, sl=sl: e.activation(A_.ap[:, sl], A_.ap[:, sl], AF.Sigmoid),
                          [A_.k(sl.start, sl.stop)], [A_.k(sl.start, sl.stop)])
                elif mg < 4:
                    g = mg - 2
                    sl = slice(jl*KB + g*512, jl*KB + (g+1)*512)
                    p.add("act", lambda e, sl=sl, bk=bk: e.activation(I_.ap[:, sl], bk.ap()[:, 0:512], AF.Sigmoid),
                          [(bk.name, "all")], [I_.k(sl.start, sl.stop)])
                else:
                    g = mg - 4
                    sl = slice(jl*KB + g*512, jl*KB + (g+1)*512)
                    p.add("act", lambda e, sl=sl, bk=bk: e.copy(V_.ap[:, sl], bk.ap()[:, 0:512]),
                          [(bk.name, "all")], [V_.k(sl.start, sl.stop)])
        for jl in range(4):
            o = jl * KB
            Ak = A_.k(o, o+KB)
            p.add("dve", lambda e, o=o: e.tensor_single_scalar(Q5.ap[:, o:o+KB], A_.ap[:, o:o+KB],
                                                               1e-10, ALU.max),
                  [Ak], [Q5.k(o, o+KB)])
            p.add("act", lambda e, o=o: e.activation(Q5.ap[:, o:o+KB], Q5.ap[:, o:o+KB], AF.Ln),
                  [Q5.k(o, o+KB)], [Q5.k(o, o+KB)])
            bks = [env.bank(), env.bank()]
            for h2 in range(2):
                env.mm(bks[h2].ap()[:, 0:512],
                       [(TRIr, Q5.ap[:, o + h2*512: o + (h2+1)*512])],
                       [CK, Q5.k(o, o+KB)], [(bks[h2].name, "all")])
            for h2 in range(2):
                dst = CD.ap[:, o + h2*512: o + (h2+1)*512]
                p.add("act", lambda e, dst=dst, bk=bks[h2]: e.activation(dst, bk.ap()[:, 0:512], AF.Exp),
                      [(bks[h2].name, "all")], [CD.k(o + h2*512, o + (h2+1)*512)])
            CDk = CD.k(o, o+KB)
            p.add("dve", lambda e, o=o: e.tensor_mul(Q5.ap[:, o:o+KB], A_.ap[:, o:o+KB], A_.ap[:, o:o+KB]),
                  [Ak], [Q5.k(o, o+KB)])
            p.add("dve", lambda e, o=o: e.tensor_scalar(Q5.ap[:, o:o+KB], Q5.ap[:, o:o+KB],
                                                        -1.0, 1.0, ALU.mult, ALU.add),
                  [Q5.k(o, o+KB)], [Q5.k(o, o+KB)])
            p.add("dve", lambda e, o=o: e.tensor_single_scalar(Q5.ap[:, o:o+KB], Q5.ap[:, o:o+KB],
                                                               1e-8, ALU.max),
                  [Q5.k(o, o+KB)], [Q5.k(o, o+KB)])
            p.add("act", lambda e, o=o: e.activation(Q5.ap[:, o:o+KB], Q5.ap[:, o:o+KB], AF.Sqrt),
                  [Q5.k(o, o+KB)], [Q5.k(o, o+KB)])
            p.add("dve", lambda e, o=o: e.tensor_mul(Q5.ap[:, o:o+KB], Q5.ap[:, o:o+KB], I_.ap[:, o:o+KB]),
                  [Q5.k(o, o+KB), I_.k(o, o+KB)], [Q5.k(o, o+KB)])
            p.add("dve", lambda e, o=o: e.tensor_mul(Q5.ap[:, o:o+KB], Q5.ap[:, o:o+KB], V_.ap[:, o:o+KB]),
                  [Q5.k(o, o+KB), V_.k(o, o+KB)], [Q5.k(o, o+KB)])
            p.add("dve", lambda e, o=o: e.tensor_single_scalar(Q6.ap[:, o:o+KB], CD.ap[:, o:o+KB],
                                                               1e-10, ALU.max),
                  [CDk], [Q6.k(o, o+KB)])
            p.add("dve", lambda e, o=o: e.reciprocal(Q6.ap[:, o:o+KB], Q6.ap[:, o:o+KB]),
                  [Q6.k(o, o+KB)], [Q6.k(o, o+KB)])
            p.add("dve", lambda e, o=o: e.tensor_mul(Q6.ap[:, o:o+KB], Q6.ap[:, o:o+KB], Q5.ap[:, o:o+KB]),
                  [Q6.k(o, o+KB), Q5.k(o, o+KB)], [Q6.k(o, o+KB)])
            bks = [env.bank(), env.bank()]
            for h2 in range(2):
                env.mm(bks[h2].ap()[:, 0:512],
                       [(TRIr, Q6.ap[:, o + h2*512: o + (h2+1)*512])],
                       [CK, Q6.k(o, o+KB)], [(bks[h2].name, "all")])
            for h2 in range(2):
                sl = slice(o + h2*512, o + (h2+1)*512)
                p.add("dve", lambda e, sl=sl, bk=bks[h2]: e.tensor_mul(
                          Q5.ap[:, sl], CD.ap[:, sl], bk.ap()[:, 0:512]),
                      [(bks[h2].name, "all"), CDk], [Q5.k(sl.start, sl.stop)])
            ibks = [env.bank(), env.bank()]
            for half, r_inc in ((0, r_incA), (1, r_incB)):
                prow = 63 if half == 0 else 127
                env.dma("qsp", r_cfs, Q5.ap[prow:prow+1, o:o+KB], [Q5.k(o, o+KB)], [RK], note="sm")
                env.dma("qsp", r_ctd, CD.ap[prow:prow+1, o:o+KB], [CDk], [RK], note="sm")
                p.add("dve", lambda e, r_inc=r_inc: e.tensor_mul(r_inc, r_ccd, r_cwc),
                      [RK], [RBK])
                p.add("dve", lambda e: e.tensor_single_scalar(r_ctd, r_ctd, 1e-10, ALU.max), [RK], [RK])
                p.add("dve", lambda e: e.tensor_mul(r_ccd, r_ccd, r_ctd), [RK], [RK])
                p.add("dve", lambda e: e.tensor_single_scalar(r_ctd, r_ccd, 1e-10, ALU.max), [RK], [RK])
                p.add("dve", lambda e: e.reciprocal(r_ctd, r_ctd), [RK], [RK])
                p.add("dve", lambda e: e.tensor_mul(r_ctd, r_ctd, r_cfs), [RK], [RK])
                p.add("dve", lambda e: e.tensor_add(r_cwc, r_cwc, r_ctd), [RK], [RK])
            for h2 in range(2):
                sl = slice(h2*512, (h2+1)*512)
                env.mm(ibks[h2].ap()[:, 0:512],
                       [(IND1, r_incA[:, sl]), (IND2, r_incB[:, sl])],
                       [CBK, RBK], [(ibks[h2].name, "all")])
            for h2 in range(2):
                sl = slice(o + h2*512, o + (h2+1)*512)
                p.add("dve", lambda e, sl=sl, bk=ibks[h2]: e.tensor_mul(
                          Q6.ap[:, sl], CD.ap[:, sl], bk.ap()[:, 0:512]),
                      [(ibks[h2].name, "all"), CDk], [Q6.k(sl.start, sl.stop)])
            if not (q == 0 and jl == 0 and os.environ.get("SKIP_CROSS0")):
                p.add("dve", lambda e, o=o: e.tensor_add(Q5.ap[:, o:o+KB], Q5.ap[:, o:o+KB], Q6.ap[:, o:o+KB]),
                      [Q5.k(o, o+KB), Q6.k(o, o+KB)], [Q5.k(o, o+KB)])
            p.add("act", lambda e, o=o: e.copy(H_.ap[:, o:o+KB], Q5.ap[:, o:o+KB]),
                  [Q5.k(o, o+KB)], [H_.k(o, o+KB)])
            if q == 0 and jl == 0 and os.environ.get("TAP1"):
                env.tap("A0", A_.ap[:, 0:KB], [A_.k(0, KB)])
                env.tap("I0", I_.ap[:, 0:KB], [I_.k(0, KB)])
                env.tap("V0", V_.ap[:, 0:KB], [V_.k(0, KB)])
                env.tap("CD0", CD.ap[:, 0:KB], [CD.k(0, KB)])
                env.tap("H0", Q5.ap[:, 0:KB], [Q5.k(0, KB)])
        env.tm2fm(H_, HF, tiles=4, feat=D)
        load_weight_N(env, w["L1_attn_W_mix"], WM, 1024, 1024)
        for jl in range(4):
            for h2 in range(2):
                bk = env.bank()
                mms = [(fmi_lhsT(HF, i, jl),
                        WM.ap[:, i*KB + h2*512: i*KB + (h2+1)*512]) for i in range(8)]
                env.mm(bk.ap()[:, 0:512], mms, [HF.k(), WM.k()], [(bk.name, "all")])
                sl = slice(h2*512, (h2+1)*512)
                p.add("dve", lambda e, sl=sl, bk=bk, h2=h2: e.tensor_add(
                          SQ.ap[:, sl], bk.ap()[:, 0:512], BMC.ap[:, h2*512:(h2+1)*512]),
                      [(bk.name, "all"), BMC.k(h2*512, (h2+1)*512)], [SQ.k(sl.start, sl.stop)])
            tiles = [t0 + jl]
            env.ln_stats(lambda t: SQ.ap[:, 0:D], lambda t: [SQ.k(0, D)], tiles=tiles)
            env.ln_apply_add(lambda t: SQ.ap[:, 0:D], lambda t: [SQ.k(0, D)], tiles=tiles)



def mx_quantize_tile(env, src_ap, src_reads, dst_ap, dst_writes, S, sc_ap, sc_k):
    p = env.prog
    C = 3 * (2.0 ** 22)
    def s_(i):
        return S.ap[:, i*1024:(i+1)*1024]
    def sk(i):
        return S.k(i*1024, (i+1)*1024)
    src3 = src_ap.rearrange("p (b j) -> p b j", b=32)
    dst3 = dst_ap.rearrange("p (b j) -> p b j", b=32)
    scm = sc_ap[:, 0:32]
    rcp = sc_ap[:, 32:64]
    def bc(ap32):
        return bass.AP(ap32.tensor, ap32.offset, [ap32.ap[0], [1, 32], [0, 32]])
    p.add("dve", lambda e: e.reduce_max(scm, src3, axis=AX.X, apply_absolute_value=True),
          src_reads, [sc_k])
    p.add("dve", lambda e: e.tensor_scalar(scm, scm, 1.0/7.5, 1e-30, ALU.mult, ALU.max),
          [sc_k], [sc_k])
    p.add("dve", lambda e: e.reciprocal(rcp, scm), [sc_k], [sc_k])
    p.add("dve", lambda e: e.tensor_tensor(s_(0).rearrange("p (b j) -> p b j", b=32),
                                           src3, bc(rcp), ALU.mult),
          src_reads + [sc_k], [sk(0)])
    for (i, k, inv) in ((1, 8.0, 0.125), (2, 4.0, 0.25), (3, 2.0, 0.5)):
        eng = "pool" if i != 1 else "dve"
        p.add(eng, lambda e, i=i, k=k: e.tensor_scalar(s_(i), s_(0), k, C, ALU.mult, ALU.add),
              [sk(0)], [sk(i)])
        p.add(eng, lambda e, i=i, inv=inv: e.tensor_scalar(s_(i), s_(i), C, inv, ALU.subtract, ALU.mult),
              [sk(i)], [sk(i)])
    p.add("pool", lambda e: e.tensor_scalar(s_(3), s_(3), 7.5, -7.5, ALU.min, ALU.max),
          [sk(3)], [sk(3)])
    p.add("act", lambda e: e.activation(s_(4), s_(0), AF.Abs), [sk(0)], [sk(4)])
    p.add("dve", lambda e: e.tensor_single_scalar(s_(0), s_(4), 2.0, ALU.is_ge),
          [sk(4)], [sk(0)])
    p.add("dve", lambda e: e.tensor_single_scalar(s_(4), s_(4), 4.0, ALU.is_ge),
          [sk(4)], [sk(4)])
    p.add("pool", lambda e: e.tensor_sub(s_(3), s_(3), s_(2)), [sk(2), sk(3)], [sk(3)])
    p.add("pool", lambda e: e.tensor_mul(s_(3), s_(3), s_(4)), [sk(3), sk(4)], [sk(3)])
    p.add("dve", lambda e: e.tensor_sub(s_(2), s_(2), s_(1)), [sk(1), sk(2)], [sk(2)])
    p.add("dve", lambda e: e.tensor_mul(s_(2), s_(2), s_(0)), [sk(0), sk(2)], [sk(2)])
    p.add("dve", lambda e: e.tensor_add(s_(1), s_(1), s_(2)), [sk(1), sk(2)], [sk(1)])
    p.add("dve", lambda e: e.tensor_add(s_(1), s_(1), s_(3)), [sk(1), sk(3)], [sk(1)])
    p.add("dve", lambda e: e.tensor_tensor(dst3, s_(1).rearrange("p (b j) -> p b j", b=32),
                                           bc(scm), ALU.mult),
          [sk(1), sc_k], dst_writes)


def slot_L2_ffn(env, w):
    p = env.prog
    KB = 1024
    XD = env.carve("l2f_xd", 0, 16 * KB, BF16)
    XDF = env.carve("l2f_xdf", 32768, 16 * KB, BF16)
    WS = env.carve("l2f_ws", 65536, 4 * KB, F32)
    WD = env.carve("l2f_wd", 81920, 8 * KB, BF16)
    WDT = env.carve("l2f_wdt", 98304, 8 * KB, BF16)
    SCR = env.carve("l2f_scr", 114688, 5 * KB, F32)
    SC2 = env.carve("l2f_sc2", 135168, 64, F32)
    XLT = env.carve("l2f_xlt", 135424, KB, F32)
    SQ = env.carve("l2f_sq", 139520, KB, F32)
    BR = env.carve("l2f_br", 135424, KB, F32)

    xap = env.x.ap()
    env.ln_stats(lambda t: xap[:, t*D:(t+1)*D], lambda t: [("x_tm", t)])
    for t in range(NTILE):
        env.ln_apply(lambda tt: xap[:, tt*D:(tt+1)*D], lambda tt: [("x_tm", tt)],
                     lambda tt: XLT.ap[:, 0:KB], lambda tt: [XLT.k()], tiles=[t])
        mx_quantize_tile(env, XLT.ap[:, 0:KB], [XLT.k()],
                         XD.ap[:, t*KB:(t+1)*KB], [XD.k(t*KB, (t+1)*KB)],
                         SCR, SC2.ap, SC2.k())
    env.tm2fm(XD, XDF)
    for hf in range(2):
        env.dma("qsp", WS.ap.rearrange("p (u m) -> p u m", u=4),
                w["L2_ffn_weight"].rearrange("(u p) m -> p u m", p=128)[:, hf*4:(hf+1)*4, :],
                [], [WS.k()], note="wload")
        for t4 in range(4):
            t = hf * 4 + t4
            mx_quantize_tile(env, WS.ap[:, t4*KB:(t4+1)*KB], [WS.k(t4*KB, (t4+1)*KB)],
                             WD.ap[:, t*KB:(t+1)*KB], [WD.k(t*KB, (t+1)*KB)],
                             SCR, SC2.ap, SC2.k())
    env.tm2fm(WD, WDT, tiles=8, feat=D)
    load_row(env, w["L2_ffn_bias"], BR.ap[0:1, 0:KB], [BR.k()])

    def fmi_lhsT(cv, kc, j, feat=D):
        o = j * feat + kc * 128
        return cv.ap[:, o:o+128]

    def fmi_rhs(cv, kc, u0, u1, feat=D):
        return cv.ap.rearrange("p (u wq) -> p u wq", wq=feat)[:, u0:u1, kc*128:(kc+1)*128]

    for t in range(NTILE):
        for h2 in range(2):
            bk = env.bank()
            mms = [(fmi_lhsT(XDF, i, t), fmi_rhs(WDT, i, h2*4, h2*4+4)) for i in range(8)]
            mms.append((env.ones_f.ap()[0:1, :], BR.ap[0:1, h2*512:(h2+1)*512]))
            env.mm(bk.ap()[:, 0:512], mms, [XDF.k(), WDT.k(), ("ones_f", "all"), BR.k()],
                   [(bk.name, "all")])
            sl = slice(h2*512, (h2+1)*512)
            p.add("act", lambda e, sl=sl, bk=bk: e.copy(SQ.ap[:, sl], bk.ap()[:, 0:512]),
                  [(bk.name, "all")], [SQ.k(sl.start, sl.stop)])
        env.ln_stats(lambda tt: SQ.ap[:, 0:D], lambda tt: [SQ.k()], tiles=[t])
        env.ln_apply_add(lambda tt: SQ.ap[:, 0:D], lambda tt: [SQ.k()], tiles=[t])



def slot_L3_ffn(env, w):
    p = env.prog
    KB = 1024
    XLN = env.carve("l3f_xln", 0, 16 * KB, BF16)
    XF = env.carve("l3f_xf", 32768, 16 * KB, BF16)
    G1C = env.carve("l3f_g1", 65536, KB, F32)
    G2C = env.carve("l3f_g2", 69632, 256, F32)
    M2 = env.carve("l3f_m2", 70656, 2 * KB, BF16)
    R1 = env.carve("l3f_r1", 74752, 2 * KB, BF16)
    R2 = env.carve("l3f_r2", 78848, 2 * KB, BF16)
    WTr = env.carve("l3f_wtr", 82944, 2 * KB, BF16)
    WTi = env.carve("l3f_wti", 87040, 2 * KB, BF16)
    WTin = env.carve("l3f_wtin", 91136, 2 * KB, BF16)
    IDT = env.carve("l3f_idt", 95232, 128, BF16)
    TMPI = env.carve("l3f_tmpi", 95488, 128, F32)
    SQ = env.carve("l3f_sq", 96000, KB, F32)
    BR = env.carve("l3f_br", 100096, 512, F32)
    BI = env.carve("l3f_bi", 102144, 512, F32)

    d1 = env.nc.dram_tensor("l3f_d1", [1024, 16, 16], BF16, kind="Internal").ap()
    d2 = env.nc.dram_tensor("l3f_d2", [512, 32, 16], BF16, kind="Internal").ap()

    load_row(env, w["L3_ffn_bias_real"], BR.ap[0:1, 0:512], [BR.k()])
    load_row(env, w["L3_ffn_bias_imag"], BI.ap[0:1, 0:512], [BI.k()])

    p.add("pool", lambda e: e.iota(TMPI.ap[:, 0:128], pattern=[[1, 128]], base=0,
                                   channel_multiplier=-1, allow_small_or_imprecise_dtypes=True),
          [], [TMPI.k()])
    p.add("dve", lambda e: e.tensor_single_scalar(IDT.ap[:, 0:128], TMPI.ap[:, 0:128],
                                                  0.0, ALU.is_equal),
          [TMPI.k()], [IDT.k()])

    def pe_t(src_ap, src_k, dst_ap, dst_k):
        bk = env.pbT
        p.add("te", lambda e: e.transpose(bk.ap()[:, 0:128], src_ap, IDT.ap[:, 0:128]),
              [src_k, IDT.k()], [("pbT", "all")])
        p.add("act", lambda e: e.copy(dst_ap, bk.ap()[:, 0:128]),
              [("pbT", "all")], [dst_k])

    def build_W(G1name, G2name, WT):
        g1src = w[G1name].rearrange("one b c d -> c b d")
        env.dma_slow("qsp", G1C.ap.rearrange("p (b dd) -> p b dd", b=32)[0:4], g1src,
                     [], [G1C.k()], note="g1")
        g2src = w[G2name].rearrange("c e one g -> c g e")
        for g in range(16):
            env.dma_slow("qsp", G2C.ap[0:4, g*16:(g+1)*16], g2src[:, g],
                         [], [G2C.k(g*16, (g+1)*16)], note="g2")
        for s in range(8):
            bk = env.bank()
            env.mm(bk.ap()[:, 0:256],
                   [(G1C.ap[0:4, s*128:(s+1)*128], G2C.ap[0:4, 0:256])],
                   [G1C.k(), G2C.k()], [(bk.name, "all")])
            p.add("act", lambda e, s=s, bk=bk: e.copy(M2.ap[:, s*256:(s+1)*256], bk.ap()[:, 0:256]),
                  [(bk.name, "all")], [M2.k(s*256, (s+1)*256)])
        env.dma_slow("qsp", d1.rearrange("(s pp) g e -> pp s g e", pp=128),
                     M2.ap.rearrange("p (s g e) -> p s g e", s=8, g=16),
                     [M2.k()], [("l3f_d1", "all")], note="b1w")
        d1r = d1.rearrange("(b dd) g e -> b g dd e", dd=32)
        for cc in range(4):
            for bl in range(8):
                env.dma_slow("qsp",
                             R1.ap[bl*16:(bl+1)*16, cc*512:(cc+1)*512].rearrange(
                                 "p (dd e) -> p dd e", dd=32),
                             d1r[cc*8 + bl],
                             [("l3f_d1", "all")], [R1.k(cc*512, (cc+1)*512)], note="b1r")
        for cc in range(4):
            for fc in range(4):
                pe_t(R1.ap[:, cc*512 + fc*128: cc*512 + (fc+1)*128],
                     R1.k(cc*512 + fc*128, cc*512 + (fc+1)*128),
                     R2.ap[:, fc*512 + cc*128: fc*512 + (cc+1)*128],
                     R2.k(fc*512 + cc*128, fc*512 + (cc+1)*128))
        env.dma_slow("qsp", d2.rearrange("(s pp) b g -> pp s b g", pp=128),
                     R2.ap.rearrange("p (s b g) -> p s b g", s=4, b=32),
                     [R2.k()], [("l3f_d2", "all")], note="b2w")
        d2r = d2.rearrange("(dd e) b g -> b e dd g", e=16)
        for cc in range(4):
            for bl in range(8):
                env.dma_slow("qsp",
                             R1.ap[bl*16:(bl+1)*16, cc*512:(cc+1)*512].rearrange(
                                 "p (dd g) -> p dd g", dd=32),
                             d2r[cc*8 + bl],
                             [("l3f_d2", "all")], [R1.k(cc*512, (cc+1)*512)], note="b2r")
        for cc in range(4):
            for fc in range(4):
                pe_t(R1.ap[:, cc*512 + fc*128: cc*512 + (fc+1)*128],
                     R1.k(cc*512 + fc*128, cc*512 + (fc+1)*128),
                     WT.ap[:, fc*512 + cc*128: fc*512 + (cc+1)*128],
                     WT.k(fc*512 + cc*128, fc*512 + (cc+1)*128))

    build_W("L3_ffn_G1_R", "L3_ffn_G2_R", WTr)
    build_W("L3_ffn_G1_I", "L3_ffn_G2_I", WTi)
    p.add("dve", lambda e: e.tensor_scalar_mul(WTin.ap[:, 0:2048], WTi.ap[:, 0:2048], -1.0),
          [WTi.k()], [WTin.k()])

    xap = env.x.ap()
    env.ln_stats(lambda t: xap[:, t*D:(t+1)*D], lambda t: [("x_tm", t)])
    env.ln_apply(lambda t: xap[:, t*D:(t+1)*D], lambda t: [("x_tm", t)],
                 lambda t: XLN.ap[:, t*D:(t+1)*D], lambda t: [XLN.k(t*D, (t+1)*D)])
    env.tm2fm(XLN, XF)

    def fmi_lhsT(cv, kc, j, feat=D):
        o = j * feat + kc * 128
        return cv.ap[:, o:o+128]

    for t in range(NTILE):
        bkr = env.bank()
        mms = [(fmi_lhsT(XF, i, t), WTr.ap[:, i*512:(i+1)*512]) for i in range(4)]
        mms += [(fmi_lhsT(XF, 4+i, t), WTin.ap[:, i*512:(i+1)*512]) for i in range(4)]
        mms.append((env.ones_f.ap()[0:1, :], BR.ap[0:1, 0:512]))
        env.mm(bkr.ap()[:, 0:512], mms, [XF.k(), WTr.k(), WTin.k(), ("ones_f", "all"), BR.k()],
               [(bkr.name, "all")])
        p.add("act", lambda e, bk=bkr: e.copy(SQ.ap[:, 0:512], bk.ap()[:, 0:512]),
              [(bkr.name, "all")], [SQ.k(0, 512)])
        bki = env.bank()
        mms = [(fmi_lhsT(XF, i, t), WTi.ap[:, i*512:(i+1)*512]) for i in range(4)]
        mms += [(fmi_lhsT(XF, 4+i, t), WTr.ap[:, i*512:(i+1)*512]) for i in range(4)]
        mms.append((env.ones_f.ap()[0:1, :], BI.ap[0:1, 0:512]))
        env.mm(bki.ap()[:, 0:512], mms, [XF.k(), WTr.k(), WTi.k(), ("ones_f", "all"), BI.k()],
               [(bki.name, "all")])
        p.add("act", lambda e, bk=bki: e.copy(SQ.ap[:, 512:1024], bk.ap()[:, 0:512]),
              [(bki.name, "all")], [SQ.k(512, 1024)])
        env.ln_stats(lambda tt: SQ.ap[:, 0:D], lambda tt: [SQ.k()], tiles=[t])
        env.ln_apply_add(lambda tt: SQ.ap[:, 0:D], lambda tt: [SQ.k()], tiles=[t])



def attn_softmax_slot(env, w, cfg):
    p = env.prog
    KB = 1024
    pref = cfg["prefix"]
    window = cfg["window"]
    scale = cfg["scale"]
    transposed = cfg["transposed_w"]

    WSLOT = env.carve(pref + "wslot", 0, 8 * KB, BF16)
    XA = env.carve(pref + "xa", 16384, 8 * KB, BF16)
    ATI = env.carve(pref + "ati", 16384, 4 * KB, BF16)
    AFI = env.carve(pref + "afi", 24576, 4 * KB, BF16)
    XF = env.carve(pref + "xf", 32768, 8 * KB, BF16)
    QF = env.carve(pref + "qf", 49152, 4 * KB, BF16)
    ET = env.carve(pref + "et", 57344, 512, BF16)
    MSC = env.carve(pref + "msc", 58368, 512, F32)
    EXB = env.carve(pref + "exb", 60416, 16, F32)
    SQ = env.carve(pref + "sq", 60480, KB, F32)
    BROW = env.carve(pref + "brow", 64576, KB, F32)
    PCB = env.carve(pref + "pcb", 68672, 16, F32)
    MSK = env.carve(pref + "msk", 68736, 4 * KB, BF16)
    if window is None:
        KF = env.carve(pref + "kf", 76928, 16 * KB, BF16)
        VT = env.carve(pref + "vt", 109696, 16 * KB, BF16)
        WQT = WOT = None
    else:
        KF = env.carve(pref + "kf", 76928, 8 * KB, BF16)
        VT = env.carve(pref + "vt", 93312, 8 * KB, BF16)
        WQT = env.carve(pref + "wqt", 109696, 8 * KB, BF16)
        WOT = env.carve(pref + "wot", 126080, 8 * KB, BF16)

    C_SHIFT = 20.0
    p.add("dve", lambda e: e.memset(EXB.ap[:, 0:1], -C_SHIFT), [], [EXB.k()])
    p.add("pool", lambda e: e.iota(MSC.ap[:, 0:128], pattern=[[1, 128]], base=0,
                                   channel_multiplier=-1, allow_small_or_imprecise_dtypes=True),
          [], [MSC.k()])
    p.add("dve", lambda e: e.tensor_single_scalar(MSC.ap[:, 128:256], MSC.ap[:, 0:128],
                                                  0.0, ALU.is_lt), [MSC.k()], [MSC.k()])
    p.add("dve", lambda e: e.tensor_scalar_mul(MSC.ap[:, 128:256], MSC.ap[:, 128:256], -1e9),
          [MSC.k()], [MSC.k()])
    p.add("dve", lambda e: e.tensor_single_scalar(MSC.ap[:, 256:384], MSC.ap[:, 0:128],
                                                  -0.5, ALU.is_gt), [MSC.k()], [MSC.k()])
    p.add("dve", lambda e: e.tensor_scalar_mul(MSC.ap[:, 256:384], MSC.ap[:, 256:384], -1e9),
          [MSC.k()], [MSC.k()])
    if window is None:
        deltas = [0, 1, 2, 3]
        def blk(dms):
            return "NEG" if dms > 0 else ("TRI" if dms == 0 else "ZERO")
        def blk2(r, s):
            return "NEG" if s < r else ("TRI" if s == r else "ZERO")
        variants = {r: [blk2(r, s) for s in range(4)] for r in deltas}
    else:
        deltas = list(range(-4, 4))
        def blk3(d, s):
            dm = d - s
            if dm == 0:
                return "TRI"
            if dm == -4:
                return "STRICT"
            if -4 < dm < 0:
                return "ZERO"
            return "NEG"
        variants = {d: [blk3(d, s) for s in range(4)] for d in deltas}
    p.add("dve", lambda e: e.memset(MSK.ap[:, 0:512*len(deltas)], 0.0), [], [MSK.k()])
    for vi, dl in enumerate(deltas):
        for s in range(4):
            kind = variants[dl][s]
            dst = MSK.ap[:, vi*512 + s*128: vi*512 + (s+1)*128]
            if kind == "NEG":
                p.add("dve", lambda e, dst=dst: e.memset(dst, -1e9), [MSK.k()], [MSK.k()])
            elif kind == "TRI":
                p.add("dve", lambda e, dst=dst: e.tensor_copy(dst, MSC.ap[:, 128:256]),
                      [MSC.k(), MSK.k()], [MSK.k()])
            elif kind == "STRICT":
                p.add("dve", lambda e, dst=dst: e.tensor_copy(dst, MSC.ap[:, 256:384]),
                      [MSC.k(), MSK.k()], [MSK.k()])
    vidx = {dl: i for i, dl in enumerate(deltas)}

    if cfg.get("bq"):
        load_vec_pc(env, w[cfg["bq"]], PCB.ap[:, 0:8], [("PCBq", "all")])
        load_vec_pc(env, w[cfg["bk"]], PCB.ap[:, 8:16], [("PCBk", "all")])

    xap = env.x.ap()
    env.ln_stats(lambda t: xap[:, t*D:(t+1)*D], lambda t: [("x_tm", t)])

    def fmi_lhsT(cv, kc, j, feat=D):
        o = j * feat + kc * 128
        return cv.ap[:, o:o+128]

    def fmi_rhs(cv, kc, u0, u1, feat=D):
        return cv.ap.rearrange("p (u wq) -> p u wq", wq=feat)[:, u0:u1, kc*128:(kc+1)*128]

    def wlhsT(cv, i, ph):
        if transposed:
            return cv.ap[:, ph*1024 + i*128: ph*1024 + (i+1)*128]
        return cv.ap[:, i*1024 + ph*128: i*1024 + (ph+1)*128]

    def wrhs(cv, i, h2):
        if transposed:
            return fmi_rhs(cv, i, h2*4, h2*4+4)
        return cv.ap[:, i*1024 + h2*512: i*1024 + (h2+1)*512]

    def loadW(name, dst):
        if transposed:
            load_weight_T(env, w[name], dst, XA, 1024, 1024)
        else:
            hbm = w[name]
            if len(hbm.shape) == 3:
                for h in range(16):
                    ph, hh = h // 2, h % 2
                    d2 = dst.ap.rearrange("p (i f) -> p i f", i=8)[
                        :, :, ph*128 + hh*64: ph*128 + (hh+1)*64]
                    s2 = hbm.rearrange("hx (i pp) a -> hx pp i a", pp=128)[h]
                    env.dma("qpool", d2, s2, [], [dst.k()], note="wqkv")
            else:
                load_weight_N(env, w[name], dst, 1024, 1024)

    accb = env.pb[6]
    nheads = 16

    for half in range(2):
        t0h = half * 8
        env.ln_apply(lambda t: xap[:, t*D:(t+1)*D], lambda t: [("x_tm", t)],
                     lambda t, t0h=t0h: XA.ap[:, (t-t0h)*D:(t-t0h+1)*D],
                     lambda t, t0h=t0h: [XA.k((t-t0h)*D, (t-t0h+1)*D)],
                     tiles=range(t0h, t0h + 8))
        env.tm2fm(XA, XF, tiles=8, feat=D)
        kwidth = 2048 if window is None else 1024
        vslots = 16 if window is None else 8

        def kproj(tg_list):
            loadW(cfg["wk"], WSLOT)
            for ph in range(8):
                for tg in tg_list:
                    bk = env.bank()
                    mms = [(wlhsT(WSLOT, i, ph),
                            XF.ap.rearrange("p (u wq) -> p u wq", wq=D)[:, tg*4:(tg+1)*4,
                                                                       i*128:(i+1)*128])
                           for i in range(8)]
                    env.mm(bk.ap()[:, 0:512], mms, [WSLOT.k(), XF.k()], [(bk.name, "all")])
                    gt = half * 2 + tg
                    o = ph*kwidth + (gt % (kwidth // 512)) * 512
                    if cfg.get("bk"):
                        p.add("dve", lambda e, o=o, bk=bk, ph=ph: e.tensor_scalar_add(
                                  KF.ap[:, o:o+512], bk.ap()[:, 0:512], PCB.ap[:, 8+ph:9+ph]),
                              [(bk.name, "all"), ("PCBk", "all")], [KF.k(o, o+512)])
                    else:
                        p.add("act", lambda e, o=o, bk=bk: e.copy(KF.ap[:, o:o+512],
                                                                  bk.ap()[:, 0:512]),
                              [(bk.name, "all")], [KF.k(o, o+512)])

        def vproj(jl_list):
            loadW(cfg["wv"], WSLOT)
            if cfg.get("bv"):
                load_row(env, w[cfg["bv"]], BROW.ap[0:1, 0:KB], [BROW.k()])
            for jl in jl_list:
                u = t0h + jl
                su = u % vslots
                for h2 in range(2):
                    bk = env.bank()
                    mms = [(fmi_lhsT(XF, i, jl), wrhs(WSLOT, i, h2)) for i in range(8)]
                    rd = [WSLOT.k(), XF.k()]
                    if cfg.get("bv"):
                        mms.append((env.ones_f.ap()[0:1, :], BROW.ap[0:1, h2*512:(h2+1)*512]))
                        rd += [("ones_f", "all"), BROW.k()]
                    env.mm(bk.ap()[:, 0:512], mms, rd, [(bk.name, "all")])
                    o = su*KB + h2*512
                    p.add("act", lambda e, o=o, bk=bk: e.copy(VT.ap[:, o:o+512],
                                                              bk.ap()[:, 0:512]),
                          [(bk.name, "all")], [VT.k(o, o+512)])

        if window is None:
            kproj([0, 1])
            vproj(list(range(8)))
        else:
            loadW(cfg["wq"], WQT)
            loadW(cfg["wo"], WOT)
        for igl in range(2):
            ig = half * 2 + igl
            if window is not None:
                kproj([igl])
                vproj([igl*4 + j for j in range(4)])
            if window is None:
                loadW(cfg["wq"], WSLOT)
                qsrc = WSLOT
            else:
                qsrc = WQT
            for ph in range(8):
                bk = env.bank()
                mms = [(wlhsT(qsrc, i, ph),
                        XF.ap.rearrange("p (u wq) -> p u wq", wq=D)[:, igl*4:(igl+1)*4,
                                                                   i*128:(i+1)*128])
                       for i in range(8)]
                env.mm(bk.ap()[:, 0:512], mms, [qsrc.k(), XF.k()], [(bk.name, "all")])
                o = ph*512
                if cfg.get("bq"):
                    p.add("dve", lambda e, o=o, bk=bk, ph=ph: e.tensor_scalar_add(
                              QF.ap[:, o:o+512], bk.ap()[:, 0:512], PCB.ap[:, ph:ph+1]),
                          [(bk.name, "all"), ("PCBq", "all")], [QF.k(o, o+512)])
                else:
                    p.add("act", lambda e, o=o, bk=bk: e.copy(QF.ap[:, o:o+512], bk.ap()[:, 0:512]),
                          [(bk.name, "all")], [QF.k(o, o+512)])
            if window is None:
                jts = list(range(0, 4*ig + 4))
            else:
                jts = [jt for jt in range(4*ig - 4, 4*ig + 4) if jt >= 0]
            for h in range(nheads):
                ph, hh = h // 2, h % 2
                for idx, jt in enumerate(jts):
                    bk = env.bank()
                    ko = ph*kwidth + (jt % (kwidth // 128)) * 128
                    env.mm(bk.ap()[:, 0:512],
                           [(KF.ap[hh*64:hh*64+64, ko:ko+128],
                             QF.ap[hh*64:hh*64+64, ph*512:(ph+1)*512])],
                           [KF.k(ko, ko+128), QF.k(ph*512, (ph+1)*512)], [(bk.name, "all")])
                    dl = jt - 4*ig
                    if dl in vidx:
                        vi = vidx[dl]
                        p.add("dve", lambda e, bk=bk, vi=vi: e.tensor_tensor(
                                  MSC.ap[:, 0:512], bk.ap()[:, 0:512],
                                  MSK.ap[:, vi*512:(vi+1)*512], ALU.add),
                              [(bk.name, "all"), MSK.k()], [MSC.k(0, 512)])
                        src_ap, src_rd = MSC.ap[:, 0:512], [MSC.k(0, 512)]
                    else:
                        src_ap, src_rd = bk.ap()[:, 0:512], [(bk.name, "all")]
                    p.add("act", lambda e, src_ap=src_ap: e.activation(
                              ET.ap[:, 0:512], src_ap, AF.Exp, bias=EXB.ap[:, 0:1],
                              scale=scale),
                          src_rd + [EXB.k()], [ET.k()])
                    vso = (jt % vslots) * KB + h * 64
                    for s in range(4):
                        env.mm(accb.ap()[:, s*66:s*66+64],
                               [(ET.ap[:, s*128:(s+1)*128], VT.ap[:, vso:vso+64])],
                               [ET.k(), VT.k(vso, vso+64)], [("pb6", "acc")],
                               start=(idx == 0 and s == 0), stop=False)
                        env.mm(accb.ap()[:, s*66+64:s*66+65],
                               [(ET.ap[:, s*128:(s+1)*128], env.ones_col.ap()[:, 0:1])],
                               [ET.k(), ("ones_col", "all")], [("pb6", "acc")],
                               start=False, stop=(idx == len(jts) - 1 and s == 3))
                for s in range(4):
                    it = 4*ig + s
                    p.add("dve", lambda e, s=s: e.reciprocal(MSC.ap[:, 0:1],
                                                             accb.ap()[:, s*66+64:s*66+65]),
                          [("pb6", "acc")], [MSC.k(0, 1)])
                    dst = ATI.ap[:, s*1024 + h*64: s*1024 + (h+1)*64]
                    p.add("dve", lambda e, s=s, dst=dst: e.tensor_scalar_mul(
                              dst, accb.ap()[:, s*66:s*66+64], MSC.ap[:, 0:1]),
                          [("pb6", "acc"), MSC.k(0, 1)],
                          [ATI.k(s*1024 + h*64, s*1024 + (h+1)*64)])
            if ig == 0 and os.environ.get("TAPA"):
                env.tap("QF", QF.ap[:, 0:4096], [QF.k()])
                env.tap("KF", KF.ap[:, 0:KF.n], [KF.k()])
                env.tap("VT", VT.ap[:, 0:VT.n], [VT.k()])
                env.tap("ATI", ATI.ap[:, 0:4096], [ATI.k()])
            env.tm2fm(ATI, AFI, tiles=4, feat=D)
            if window is None:
                loadW(cfg["wo"], WSLOT)
                osrc = WSLOT
            else:
                osrc = WOT
            if cfg.get("bo"):
                load_row(env, w[cfg["bo"]], BROW.ap[0:1, 0:KB], [BROW.k()])
            for s in range(4):
                t = 4*ig + s
                for h2 in range(2):
                    bk = env.bank()
                    mms = [(fmi_lhsT(AFI, i, s), wrhs(osrc, i, h2)) for i in range(8)]
                    rd = [AFI.k(), osrc.k()]
                    if cfg.get("bo"):
                        mms.append((env.ones_f.ap()[0:1, :], BROW.ap[0:1, h2*512:(h2+1)*512]))
                        rd += [("ones_f", "all"), BROW.k()]
                    env.mm(bk.ap()[:, 0:512], mms, rd, [(bk.name, "all")])
                    sl = slice(h2*512, (h2+1)*512)
                    p.add("act", lambda e, sl=sl, bk=bk: e.copy(SQ.ap[:, sl], bk.ap()[:, 0:512]),
                          [(bk.name, "all")], [SQ.k(sl.start, sl.stop)])
                if t == 0 and os.environ.get("TAPA"):
                    env.tap("SQ0", SQ.ap[:, 0:D], [SQ.k()])
                env.ln_stats(lambda tt: SQ.ap[:, 0:D], lambda tt: [SQ.k()], tiles=[t])
                env.ln_apply_add(lambda tt: SQ.ap[:, 0:D], lambda tt: [SQ.k()], tiles=[t])


def slot_L2_attn(env, w):
    attn_softmax_slot(env, w, dict(
        prefix="l2a_", window=None, scale=0.125, transposed_w=False,
        wq="L2_attn_w_q", wk="L2_attn_w_k", wv="L2_attn_w_v", wo="L2_attn_w_o",
        bq=None, bk=None, bv=None, bo=None))


def slot_L3_attn(env, w):
    attn_softmax_slot(env, w, dict(
        prefix="l3a_", window=512, scale=0.125, transposed_w=True,
        wq="L3_attn_Wq", wk="L3_attn_Wk", wv="L3_attn_Wv", wo="L3_attn_Wo",
        bq="L3_attn_bq", bk="L3_attn_bk", bv="L3_attn_bv", bo="L3_attn_bo"))



def slot_L1_ffn(env, w):
    p = env.prog
    KB = 1024
    XF = env.carve("l1f_xf", 0, 16 * KB, BF16)
    XLN = env.carve("l1f_xln", 32768, 16 * KB, BF16)
    SACC = env.carve("l1f_sacc", 32768, 16 * KB, F32)
    RUT = env.carve("l1f_rut", 98304, 4 * KB, BF16)
    RDT = env.carve("l1f_rdt", 106496, 4 * KB, BF16)
    WST = env.carve("l1f_wst", 114688, 4 * KB, BF16)
    ACTQ = env.carve("l1f_actq", 122880, 4 * KB, BF16)
    GT = env.carve("l1f_gt", 131072, 256, F32)
    PR = env.carve("l1f_pr", 132096, 64, F32)
    RT = env.carve("l1f_rt", 132352, 128, BF16)

    env.dma("qpool", WST.ap[0:16, 0:1024], w["L1_ffn_router_w"], [], [WST.k()], note="rw")
    env.xpose(RT.ap[:, 0:128].rearrange("p (m t) -> p m t", t=16),
              WST.ap[0:16, 0:1024], [WST.k()], [RT.k()])

    xap = env.x.ap()
    env.ln_stats(lambda t: xap[:, t*D:(t+1)*D], lambda t: [("x_tm", t)])
    env.ln_apply(lambda t: xap[:, t*D:(t+1)*D], lambda t: [("x_tm", t)],
                 lambda t: XLN.ap[:, t*D:(t+1)*D], lambda t: [XLN.k(t*D, (t+1)*D)])
    env.tm2fm(XLN, XF)

    def fmi_lhsT(cv, kc, j, feat=D):
        o = j * feat + kc * 128
        return cv.ap[:, o:o+128]

    def fmi_rhs(cv, kc, u0, u1, feat=D):
        return cv.ap.rearrange("p (u wq) -> p u wq", wq=feat)[:, u0:u1, kc*128:(kc+1)*128]

    for t in range(NTILE):
        bk = env.bank()
        mms = [(fmi_lhsT(XF, i, t), RT.ap[:, i*16:(i+1)*16]) for i in range(8)]
        env.mm(bk.ap()[:, 0:16], mms, [XF.k(), RT.k()], [(bk.name, "all")])
        pr = PR.ap[:, 0:16]; sc = PR.ap[:, 16:32]; m1 = PR.ap[:, 32:33]
        rs = PR.ap[:, 33:34]; eq = PR.ap[:, 34:50]
        PK = PR.k()
        p.add("dve", lambda e, bk=bk: e.reduce_max(m1, bk.ap()[:, 0:16], axis=AX.X,
                                                   negate=True),
              [(bk.name, "all")], [PK])
        p.add("act", lambda e, bk=bk: e.activation(pr, bk.ap()[:, 0:16], AF.Exp,
                                                   bias=m1, accum_out=rs),
              [(bk.name, "all"), PK], [PK])
        p.add("dve", lambda e: e.reciprocal(rs, rs), [PK], [PK])
        p.add("dve", lambda e: e.tensor_scalar_mul(pr, pr, rs), [PK], [PK])
        p.add("dve", lambda e: e.tensor_copy(sc, pr), [PK], [PK])
        for it in range(3):
            p.add("dve", lambda e: e.reduce_max(m1, sc, axis=AX.X), [PK], [PK])
            p.add("dve", lambda e: e.tensor_scalar(eq, sc, m1, None, ALU.is_equal),
                  [PK], [PK])
            p.add("dve", lambda e: e.scalar_tensor_tensor(sc, eq, -2.0, sc, ALU.mult, ALU.add),
                  [PK], [PK])
        p.add("dve", lambda e: e.reduce_max(m1, sc, axis=AX.X), [PK], [PK])
        p.add("dve", lambda e: e.tensor_scalar(eq, pr, m1, None, ALU.is_ge), [PK], [PK])
        gsl = GT.ap[:, t*16:(t+1)*16]
        p.add("dve", lambda e, gsl=gsl: e.tensor_mul(gsl, pr, eq), [PK], [GT.k(t*16, (t+1)*16)])

    def expert(kind, idx):
        if kind == "shared":
            up_w = w["L1_ffn_shared_up"][idx]
            dn_w = w["L1_ffn_shared_down"][idx]
        else:
            up_w = w["L1_ffn_routed_up"][idx]
            dn_w = w["L1_ffn_routed_down"][idx]
        load_weight_T(env, up_w, RUT, WST, 512, 1024)
        load_weight_T(env, dn_w, RDT, WST, 1024, 512)
        first = (kind == "shared" and idx == 0)
        for q in range(4):
            t0 = q * 4
            for ec in range(4):
                for tg2 in range(2):
                    bk = env.bank()
                    mms = [(RUT.ap[:, ec*1024 + kc*128: ec*1024 + (kc+1)*128],
                            fmi_rhs(XF, kc, t0 + tg2*2, t0 + tg2*2 + 2))
                           for kc in range(8)]
                    env.mm(bk.ap()[:, 0:256], mms, [RUT.k(), XF.k()], [(bk.name, "all")])
                    dst = ACTQ.ap.rearrange("p (u wq) -> p u wq", wq=KB)[
                        :, tg2*2:(tg2+1)*2, ec*128:(ec+1)*128]
                    p.add("act", lambda e, dst=dst, bk=bk: e.activation(
                              dst, bk.ap()[:, 0:256].rearrange("p (u tt) -> p u tt", tt=128),
                              AF.Gelu),
                          [(bk.name, "all")], [ACTQ.k()])
            for u in range(4):
                for h2 in range(2):
                    bk = env.bank()
                    mms = [(ACTQ.ap[:, u*KB + ec*128: u*KB + (ec+1)*128],
                            fmi_rhs(RDT, ec, h2*4, h2*4+4, feat=512))
                           for ec in range(4)]
                    env.mm(bk.ap()[:, 0:512], mms, [ACTQ.k(), RDT.k()], [(bk.name, "all")])
                    t = t0 + u
                    so = t*KB + h2*512
                    ssl = SACC.ap[:, so:so+512]
                    if first:
                        p.add("act", lambda e, ssl=ssl, bk=bk: e.copy(ssl, bk.ap()[:, 0:512]),
                              [(bk.name, "all")], [SACC.k(so, so+512)])
                    elif kind == "shared":
                        p.add("dve", lambda e, ssl=ssl, bk=bk: e.tensor_add(
                                  ssl, ssl, bk.ap()[:, 0:512]),
                              [(bk.name, "all"), SACC.k(so, so+512)], [SACC.k(so, so+512)])
                    else:
                        gcol = GT.ap[:, t*16 + idx: t*16 + idx + 1]
                        p.add("dve", lambda e, ssl=ssl, bk=bk, gcol=gcol: e.scalar_tensor_tensor(
                                  ssl, bk.ap()[:, 0:512], gcol, ssl, ALU.mult, ALU.add),
                              [(bk.name, "all"), SACC.k(so, so+512), GT.k(t*16, (t+1)*16)],
                              [SACC.k(so, so+512)])

    expert("shared", 0)
    expert("shared", 1)
    for r in range(16):
        expert("routed", r)

    env.ln_stats(lambda t: SACC.ap[:, t*D:(t+1)*D], lambda t: [SACC.k(t*D, (t+1)*D)])
    env.ln_apply_add(lambda t: SACC.ap[:, t*D:(t+1)*D],
                     lambda t: [SACC.k(t*D, (t+1)*D)])



def slot_L0_attn(env, w, lo=0, hi=6, first=True, last=True,
                 aux_in=None, aux_out=None):
    p = env.prog
    KB = 1024
    OUT = env.carve("l0_out", 0, 16 * KB, F32)
    OFB = env.carve("l0_ofb", 65536, 8 * KB, BF16)
    WSLOT = env.carve("l0_ws", 65536, 8 * KB, BF16)
    OF = env.carve("l0_of", 81920, 8 * KB, BF16)
    QP = env.carve("l0_qp", 98304, 2 * KB, BF16)
    KP = env.carve("l0_kp", 102400, 2 * KB, BF16)
    PHQ = env.carve("l0_phq", 106496, 2 * KB, BF16)
    PHK = env.carve("l0_phk", 110592, 2 * KB, BF16)
    PKT = env.carve("l0_pkt", 114688, 2 * KB, BF16)
    AFE = env.carve("l0_afe", 114688, 2 * KB, BF16)
    VTE = env.carve("l0_vte", 118784, 2080, BF16)
    ATE = env.carve("l0_ate", 122944, 2 * KB, BF16)
    MSKD = env.carve("l0_mskd", 127040, 512, BF16)
    Q2S = env.carve("l0_q2s", 128064, 256, BF16)
    KVS = env.carve("l0_kvs", 128576, 1040, F32)
    KVB = env.carve("l0_kvb", 132736, 1040, BF16)
    OMG = env.carve("l0_omg", 134848, 256, BF16)
    NRM = env.carve("l0_nrm", 135360, 512, BF16)
    BVR = env.carve("l0_bvr", 136384, KB, BF16)
    G2B = env.carve("l0_g2b", 138432, KB, BF16)
    BRWA = env.carve("l0_brwa", 140480, KB, BF16)
    W1B = env.carve("l0_w1b", 98304, 4 * KB, BF16)
    W2B = env.carve("l0_w2b", 106496, 8 * KB, BF16)
    YB = env.carve("l0_yb", 122880, 8 * KB, BF16)
    G1B = env.carve("l0_g1b", 139264, KB, BF16)
    BRWF = env.carve("l0_brwf", 141312, KB, BF16)
    CON = env.carve("l0_con", 143360, 352, BF16)
    PCB = env.carve("l0_pcb", 144064, 32, F32)
    PHB = env.carve("l0_phb", 144192, 16, F32)
    BF1C = env.carve("l0_bf1c", 144256, 32, F32)
    RCP = env.carve("l0_rcp", 144384, 16, F32)

    xap = env.x.ap()
    if first:
        env.ln_stats(lambda t: xap[:, t*D:(t+1)*D], lambda t: [("x_tm", t)])
        env.ln_apply(lambda t: xap[:, t*D:(t+1)*D], lambda t: [("x_tm", t)],
                     lambda t: OUT.ap[:, t*D:(t+1)*D], lambda t: [OUT.k(t*D, (t+1)*D)])
    else:
        env.dma("qsp", OUT.ap.rearrange("p (u f) -> p u f", u=NTILE),
                aux_in.rearrange("(u pp) f -> pp u f", pp=128),
                [], [OUT.k()], note="auxin")

    TRI = CON.ap[:, 0:128]
    ONEB65 = CON.ap[:, 128:193]
    ONE64B = CON.ap[64:65, 193:321]
    CK = CON.k()
    p.add("pool", lambda e: e.iota(TRI, pattern=[[1, 128]], base=0, channel_multiplier=-1,
                                   allow_small_or_imprecise_dtypes=True), [], [CK])
    p.add("dve", lambda e: e.tensor_single_scalar(TRI, TRI, -0.5, ALU.is_gt), [CK], [CK])
    p.add("dve", lambda e: e.memset(ONEB65, 0.0), [], [CK])
    p.add("pool", lambda e: e.iota(ONEB65[:, 0:1], pattern=[[0, 1]], base=0,
                                   channel_multiplier=1,
                                   allow_small_or_imprecise_dtypes=True), [CK], [CK])
    p.add("pool", lambda e: e.iota(ONEB65[:, 64:65], pattern=[[0, 1]], base=0,
                                   channel_multiplier=1,
                                   allow_small_or_imprecise_dtypes=True), [CK], [CK])
    p.add("dve", lambda e: e.tensor_single_scalar(ONEB65[:, 0:1], ONEB65[:, 0:1], 63.5,
                                                  ALU.is_le), [CK], [CK])
    p.add("dve", lambda e: e.tensor_single_scalar(ONEB65[:, 64:65], ONEB65[:, 64:65], 63.5,
                                                  ALU.is_gt), [CK], [CK])
    p.add("dve", lambda e: e.memset(ONE64B, 1.0), [], [CK])
    p.add("dve", lambda e: e.memset(PHB.ap[:, 0:1], -0.5 * math.log(128.0)), [], [PHB.k()])

    DHS = 64.0 ** (-0.25)

    def fmi_lhsT(cv, kc, j):
        o = j * D + kc * 128
        return cv.ap[:, o:o+128]

    def fmi_rhs(cv, kc, u0, u1):
        return cv.ap.rearrange("p (u wq) -> p u wq", wq=D)[:, u0:u1, kc*128:(kc+1)*128]

    ROWA = OF.ap.bitcast(F32)[0:1, 0:1024]
    ROWB = OF.ap.bitcast(F32)[0:1, 1024:2048]
    RKA = OF.k(0, 2048)
    RKB = OF.k(2048, 4096)

    def bcast_tile(dst, src_hbm, note):
        env.dma("qsp", ROWA, src_hbm[None, :], [], [RKA], note=note)
        for h2 in range(2):
            bk = env.bank()
            env.mm(bk.ap()[:, 0:512],
                   [(env.ones_f.ap()[0:1, :], ROWA[:, h2*512:(h2+1)*512])],
                   [("ones_f", "all"), RKA], [(bk.name, "all")])
            p.add("act", lambda e, h2=h2, bk=bk, dst=dst: e.copy(
                      dst.ap[:, h2*512:(h2+1)*512], bk.ap()[:, 0:512]),
                  [(bk.name, "all")], [dst.k(h2*512, (h2+1)*512)])

    def sum_row(dst, v0, v1, note):
        env.dma("qsp", ROWA, v0[None, :], [], [RKA], note=note)
        env.dma("qsp", ROWB, v1[None, :], [], [RKB], note=note)
        p.add("dve", lambda e, dst=dst: e.tensor_add(dst.ap[0:1, 0:1024], ROWA, ROWB),
              [RKA, RKB], [dst.k()])

    def copy_row(dst, v0, note):
        env.dma("qsp", ROWA, v0[None, :], [], [RKA], note=note)
        p.add("dve", lambda e, dst=dst: e.tensor_copy(dst.ap[0:1, 0:1024], ROWA),
              [RKA], [dst.k()])

    for l in range(lo, hi):
        aff = l > lo
        p.add("dve", lambda e: e.memset(OMG.ap[:, 0:256], 0.0), [], [OMG.k()])
        for hh in range(2):
            env.dma("qpool", OMG.ap[hh*64:(hh+1)*64, hh*128:hh*128+64],
                    w["L0_attn_omega"][l], [], [OMG.k()], note="omg")
        for hh in range(2):
            p.add("dve", lambda e, hh=hh: e.tensor_scalar_mul(
                      OMG.ap[hh*64:(hh+1)*64, hh*128+64:hh*128+128],
                      OMG.ap[hh*64:(hh+1)*64, hh*128:hh*128+64], -1.0),
                  [OMG.k()], [OMG.k()])
        load_vec_pc(env, w["L0_attn_bq"][l], PCB.ap[:, 0:8], [PCB.k(0, 8)])
        load_vec_pc(env, w["L0_attn_bk"][l], PCB.ap[:, 8:16], [PCB.k(8, 16)])
        copy_row(BVR, w["L0_attn_bv"][l], "bv")
        if aff:
            load_vec_pc(env, w["L0_attn_g2"][l-1], PCB.ap[:, 16:24], [PCB.k(16, 24)])
            load_vec_pc(env, w["L0_attn_be2"][l-1], PCB.ap[:, 24:32], [PCB.k(24, 32)])
            bcast_tile(G2B, w["L0_attn_g2"][l-1], "g2t")
            sum_row(BRWA, w["L0_attn_be2"][l-1], w["L0_attn_bo"][l], "brwa")
        else:
            copy_row(BRWA, w["L0_attn_bo"][l], "bo")
        p.add("dve", lambda e: e.memset(KVS.ap[:, 0:1040], 0.0), [], [KVS.k()])
        p.add("dve", lambda e: e.memset(KVB.ap[:, 0:1040], 0.0), [], [KVB.k()])

        for half in range(2):
            t0h = half * 8
            for tl in range(8):
                t = t0h + tl
                p.add("dve", lambda e, t=t, tl=tl: e.tensor_copy(
                          OFB.ap[:, tl*D:(tl+1)*D], OUT.ap[:, t*D:(t+1)*D]),
                      [OUT.k(t*D, (t+1)*D)], [OFB.k(tl*D, (tl+1)*D)])
            env.tm2fm(OFB, OF, tiles=8, feat=D)
            if aff:
                for kc in range(8):
                    p.add("dve", lambda e, kc=kc: e.tensor_scalar(
                              fmi_rhs(OF, kc, 0, 8), fmi_rhs(OF, kc, 0, 8),
                              PCB.ap[:, 16+kc:17+kc], PCB.ap[:, 24+kc:25+kc],
                              ALU.mult, ALU.add),
                          [OF.k(), PCB.k(16, 32)], [OF.k()])
            dbg = os.environ.get("L0DBG", "")
            for ei in range(4 if "noeighth" not in dbg else 0):
                el = ei * 2
                for (wn, dstc, bc) in (("L0_attn_Wq", QP, 0), ("L0_attn_Wk", KP, 8)):
                    load_weight_N(env, w[wn][l], WSLOT, 1024, 1024)
                    for ph in range(8):
                        bk = env.bank()
                        mms = [(WSLOT.ap[:, i*KB + ph*128: i*KB + (ph+1)*128],
                                fmi_rhs(OF, i, el, el+2)) for i in range(8)]
                        env.mm(bk.ap()[:, 0:256], mms, [WSLOT.k(), OF.k()],
                               [(bk.name, "all")])
                        o = ph * 256
                        p.add("dve", lambda e, o=o, bk=bk, ph=ph, bc=bc, dstc=dstc:
                              e.tensor_scalar(dstc.ap[:, o:o+256], bk.ap()[:, 0:256],
                                              PCB.ap[:, bc+ph:bc+ph+1], DHS,
                                              ALU.add, ALU.mult),
                              [(bk.name, "all"), PCB.k(bc, bc+8)], [dstc.k(o, o+256)])
                vones = VTE.ap.rearrange("p (m a) -> p m a", a=65)[:, :, 64:65]
                p.add("dve", lambda e, vones=vones: e.memset(vones, 1.0), [], [VTE.k()])
                load_weight_N(env, w["L0_attn_Wv"][l], WSLOT, 1024, 1024)
                for jl in range(2):
                    for h2 in range(2):
                        bk = env.bank()
                        mms = [(fmi_lhsT(OF, i, el + jl),
                                WSLOT.ap[:, i*KB + h2*512: i*KB + (h2+1)*512])
                               for i in range(8)]
                        mms.append((env.ones_bf.ap()[0:1, :],
                                    BVR.ap[0:1, h2*512:(h2+1)*512]))
                        env.mm(bk.ap()[:, 0:512], mms,
                               [WSLOT.k(), OF.k(), ("ones_bf", "all"), BVR.k()],
                               [(bk.name, "all")])
                        dst = VTE.ap[:, jl*1040 + h2*520: jl*1040 + (h2+1)*520]\
                            .rearrange("p (h a) -> p h a", a=65)[:, :, 0:64]
                        p.add("dve", lambda e, dst=dst, bk=bk: e.tensor_copy(
                                  dst, bk.ap()[:, 0:512].rearrange("p (h a) -> p h a",
                                                                   a=64)),
                              [(bk.name, "all")], [VTE.k(jl*1040, (jl+1)*1040)])
                if os.environ.get("TAPL0") and l == lo and half == 0 and ei == 0:
                    env.tap("qp", QP.ap[:, 0:2048], [QP.k()])
                    env.tap("kp", KP.ap[:, 0:2048], [KP.k()])
                    env.tap("vte", VTE.ap[:, 0:2080], [VTE.k()])
                for cl in range(2 if "nochunk" not in dbg else 0):
                    for (srcc, dstc) in (((QP, PHQ), (KP, PHK))
                                         if "nophi" not in dbg else ()):
                        for grp in range(4):
                            if "nonrm" not in dbg:
                                a0 = srcc.ap[:, grp*512 + cl*128: grp*512 + cl*128 + 128]
                                sq3 = bass.AP(a0.tensor, a0.offset,
                                              [a0.ap[0], [256, 2], [1, 128]])
                                q23 = Q2S.ap[:, 0:256].rearrange("p (u t) -> p u t",
                                                                 t=128)
                                p.add("act", lambda e, sq3=sq3, q23=q23: e.activation(
                                          q23, sq3, AF.Square),
                                      [srcc.k(grp*512, (grp+1)*512)], [Q2S.k()])
                                bkn = env.bank()
                                p.add("te", [
                                    (lambda e, bkn=bkn: e.matmul(
                                        bkn.ap()[0:1, 0:256], ONEB65[:, 0:1],
                                        Q2S.ap[:, 0:256], start=True, stop=False,
                                        skip_group_check=True)),
                                    (lambda e, bkn=bkn: e.matmul(
                                        bkn.ap()[0:1, 256:512], ONEB65[:, 64:65],
                                        Q2S.ap[:, 0:256], start=False, stop=True,
                                        skip_group_check=True)),
                                ], [CK, Q2S.k()], [(bkn.name, "all")])
                                p.add("dve", lambda e, bkn=bkn: e.tensor_scalar_mul(
                                          NRM.ap[0:1, 0:512], bkn.ap()[0:1, 0:512],
                                          -0.5),
                                      [(bkn.name, "all")], [NRM.k()])
                            bku = env.bank()
                            ues = []
                            for s4 in range(4):
                                pl, hh = s4 // 2, s4 % 2
                                ph = grp*2 + pl
                                qsl = srcc.ap[:, ph*256 + cl*128: ph*256 + (cl+1)*128]
                                ues.append(lambda e, s4=s4, hh=hh, qsl=qsl, bku=bku,
                                           st=(s4 == 3 and "noaug" in dbg): e.matmul(
                                    bku.ap()[:, s4*128:(s4+1)*128],
                                    OMG.ap[:, hh*128:(hh+1)*128], qsl,
                                    start=(s4 == 0), stop=st,
                                    skip_group_check=True))
                            p.add("te", ues, [OMG.k(), srcc.k(grp*512, (grp+1)*512)],
                                  [(bku.name, "all")])
                            if "noaug" not in dbg:
                                aes = []
                                for s4 in range(4):
                                    pl, hh = s4 // 2, s4 % 2
                                    nsl = NRM.ap[0:1, hh*256 + pl*128:
                                                 hh*256 + (pl+1)*128]
                                    aes.append(lambda e, s4=s4, nsl=nsl,
                                               bku=bku: e.matmul(
                                        bku.ap()[:, s4*128:(s4+1)*128],
                                        env.ones_bf.ap()[0:1, 0:128], nsl,
                                        start=False, stop=(s4 == 3),
                                        skip_group_check=True))
                                p.add("te", aes, [NRM.k(), ("ones_bf", "all")],
                                      [(bku.name, "all")])
                            dst = dstc.ap[:, grp*512:(grp+1)*512]
                            p.add("act", lambda e, dst=dst, bku=bku: e.activation(
                                      dst, bku.ap()[:, 0:512], AF.Exp,
                                      bias=PHB.ap[:, 0:1], scale=1.0),
                                  [(bku.name, "all"), PHB.k()],
                                  [dstc.k(grp*512, (grp+1)*512)])
                    if (os.environ.get("TAPL0") and l == lo and half == 0 and ei == 0
                            and cl == 0):
                        env.tap("phq", PHQ.ap[:, 0:2048], [PHQ.k()])
                        env.tap("phk", PHK.ap[:, 0:2048], [PHK.k()])
                    env.xpose(PKT.ap[:, 0:2048].rearrange("p (m t) -> p m t", t=128),
                              PHK.ap[:, 0:2048], [PHK.k()], [PKT.k()])
                    for grp in range(4 if "nosc" not in dbg else 0):
                        bks = env.bank()
                        for s4 in range(4):
                            h = grp*4 + s4
                            env.mm(bks.ap()[:, s4*128:(s4+1)*128],
                                   [(PHK.ap[:, h*128:(h+1)*128],
                                     PHQ.ap[:, h*128:(h+1)*128])],
                                   [PHK.k(), PHQ.k()], [(bks.name, "all")],
                                   start=(s4 == 0), stop=(s4 == 3))
                        trib = bass.AP(TRI.tensor, TRI.offset,
                                       [TRI.ap[0], [0, 4], [1, 128]])
                        msk3 = MSKD.ap[:, 0:512].rearrange("p (x t) -> p x t", t=128)
                        p.add("dve", lambda e, bks=bks, trib=trib, msk3=msk3:
                              e.tensor_tensor(
                                  msk3,
                                  bks.ap()[:, 0:512].rearrange("p (x t) -> p x t", t=128),
                                  trib, ALU.mult),
                              [(bks.name, "all"), CK], [MSKD.k()])
                        bkn2 = env.bank()
                        for s4 in range(4):
                            h = grp*4 + s4
                            mnd = [(MSKD.ap[:, s4*128:(s4+1)*128],
                                    VTE.ap[:, cl*1040 + h*65: cl*1040 + (h+1)*65]),
                                   (PHQ.ap[:, h*128:(h+1)*128],
                                    KVB.ap[:, h*65:(h+1)*65])]
                            env.mm(bkn2.ap()[:, s4*65:(s4+1)*65], mnd,
                                   [MSKD.k(), VTE.k(cl*1040, (cl+1)*1040), PHQ.k(),
                                    KVB.k()],
                                   [(bkn2.name, "all")], start=(s4 == 0), stop=(s4 == 3))
                        b0 = bkn2.ap()[:, 0:1]
                        num3 = bass.AP(b0.tensor, b0.offset, [b0.ap[0], [65, 4], [1, 64]])
                        b64 = bkn2.ap()[:, 64:65]
                        den3 = bass.AP(b64.tensor, b64.offset, [b64.ap[0], [65, 4], [1, 1]])
                        r3 = RCP.ap[:, 0:4].rearrange("p (a b) -> p a b", b=1)
                        p.add("dve", lambda e, den3=den3, r3=r3: e.tensor_scalar_add(
                                  r3, den3, 1e-6), [(bkn2.name, "all")], [RCP.k()])
                        p.add("dve", lambda e: e.reciprocal(RCP.ap[:, 0:4],
                                                            RCP.ap[:, 0:4]),
                              [RCP.k()], [RCP.k()])
                        r0 = RCP.ap[:, 0:1]
                        rcpb = bass.AP(r0.tensor, r0.offset, [r0.ap[0], [1, 4], [0, 64]])
                        dsta = ATE.ap[:, cl*KB + grp*256: cl*KB + (grp+1)*256]\
                            .rearrange("p (h a) -> p h a", a=64)
                        p.add("dve", lambda e, num3=num3, rcpb=rcpb, dsta=dsta:
                              e.tensor_tensor(dsta, num3, rcpb, ALU.mult),
                              [(bkn2.name, "all"), RCP.k()],
                              [ATE.k(cl*KB + grp*256, cl*KB + (grp+1)*256)])

                    for pg in range(4 if "nokv" not in dbg else 0):
                        bkv = env.bank()
                        kes = []
                        for s4 in range(4):
                            h = pg*4 + s4
                            lhs = PKT.ap[:, h*128:(h+1)*128]
                            rhs = VTE.ap[:, cl*1040 + h*65: cl*1040 + (h+1)*65]
                            kes.append(lambda e, lhs=lhs, rhs=rhs, s4=s4, bkv=bkv:
                                       e.matmul(
                                bkv.ap()[:, s4*65:(s4+1)*65], lhs, rhs,
                                start=(s4 == 0), stop=(s4 == 3),
                                skip_group_check=True))
                        p.add("te", kes, [PKT.k(), VTE.k(cl*1040, (cl+1)*1040)],
                              [(bkv.name, "all")])
                        lo_s = pg * 260
                        p.add("dve", lambda e, lo_s=lo_s, bkv=bkv: e.tensor_add(
                                  KVS.ap[:, lo_s:lo_s+260], KVS.ap[:, lo_s:lo_s+260],
                                  bkv.ap()[:, 0:260]),
                              [(bkv.name, "all"), KVS.k(lo_s, lo_s+260)],
                              [KVS.k(lo_s, lo_s+260)])
                        p.add("dve", lambda e, lo_s=lo_s: e.tensor_copy(
                                  KVB.ap[:, lo_s:lo_s+260], KVS.ap[:, lo_s:lo_s+260]),
                              [KVS.k(lo_s, lo_s+260)], [KVB.k(lo_s, lo_s+260)])
                if os.environ.get("TAPL0") and l == lo and half == 0 and ei == 0:
                    env.tap("ate", ATE.ap[:, 0:2048], [ATE.k()])
                    env.tap("kvs", KVS.ap[:, 0:1040], [KVS.k()])
                env.xpose(AFE.ap[:, 0:2048].rearrange("p (m t) -> p m t", t=128),
                          ATE.ap[:, 0:2048], [ATE.k()], [AFE.k()])
                load_weight_N(env, w["L0_attn_Wo"][l], WSLOT, 1024, 1024)
                for sl2 in range(2):
                    t = t0h + el + sl2
                    for h2 in range(2):
                        bk = env.bank()
                        mms = [(fmi_lhsT(AFE, i, sl2),
                                WSLOT.ap[:, i*KB + h2*512: i*KB + (h2+1)*512])
                               for i in range(8)]
                        mms.append((env.ones_bf.ap()[0:1, :],
                                    BRWA.ap[0:1, h2*512:(h2+1)*512]))
                        env.mm(bk.ap()[:, 0:512], mms,
                               [AFE.k(), WSLOT.k(), ("ones_bf", "all"), BRWA.k()],
                               [(bk.name, "all")])
                        so = t*D + h2*512
                        if aff:
                            p.add("dve", lambda e, so=so, h2=h2: e.tensor_mul(
                                      OUT.ap[:, so:so+512], OUT.ap[:, so:so+512],
                                      G2B.ap[:, h2*512:(h2+1)*512]),
                                  [OUT.k(so, so+512), G2B.k()], [OUT.k(so, so+512)])
                        p.add("dve", lambda e, so=so, bk=bk: e.tensor_add(
                                  OUT.ap[:, so:so+512], OUT.ap[:, so:so+512],
                                  bk.ap()[:, 0:512]),
                              [(bk.name, "all"), OUT.k(so, so+512)],
                              [OUT.k(so, so+512)])
        if os.environ.get("TAPL0") and l == lo:
            env.tap("z1", OUT.ap[:, 0:4096], [OUT.k(0, 4096)])
        env.ln_stats(lambda t: OUT.ap[:, t*D:(t+1)*D], lambda t: [OUT.k(t*D, (t+1)*D)])
        env.ln_apply(lambda t: OUT.ap[:, t*D:(t+1)*D], lambda t: [OUT.k(t*D, (t+1)*D)],
                     lambda t: OUT.ap[:, t*D:(t+1)*D], lambda t: [OUT.k(t*D, (t+1)*D)])
        load_vec_pc(env, w["L0_attn_g1"][l], PCB.ap[:, 16:24], [PCB.k(16, 24)])
        load_vec_pc(env, w["L0_attn_be1"][l], PCB.ap[:, 24:32], [PCB.k(24, 32)])
        load_vec_pc(env, w["L0_attn_bf1"][l], BF1C.ap[:, 0:32], [BF1C.k()], nchunk=32)
        bcast_tile(G1B, w["L0_attn_g1"][l], "g1t")
        sum_row(BRWF, w["L0_attn_be1"][l], w["L0_attn_bf2"][l], "brwf")
        for half in range(2):
            t0h = half * 8
            for tl in range(8):
                t = t0h + tl
                p.add("dve", lambda e, t=t, tl=tl: e.tensor_copy(
                          OFB.ap[:, tl*D:(tl+1)*D], OUT.ap[:, t*D:(t+1)*D]),
                      [OUT.k(t*D, (t+1)*D)], [OFB.k(tl*D, (tl+1)*D)])
            env.tm2fm(OFB, OF, tiles=8, feat=D)
            for kc in range(8):
                p.add("dve", lambda e, kc=kc: e.tensor_scalar(
                          fmi_rhs(OF, kc, 0, 8), fmi_rhs(OF, kc, 0, 8),
                          PCB.ap[:, 16+kc:17+kc], PCB.ap[:, 24+kc:25+kc],
                          ALU.mult, ALU.add),
                      [OF.k(), PCB.k(16, 32)], [OF.k()])
            for b in range(4):
                for sub in range(2):
                    fcb = b*8 + sub*4
                    env.dma("qpool", W1B.ap.rearrange("p (u f) -> p u f", u=8),
                            w["L0_attn_W1"][l].rearrange("(u pp) f -> pp u f", pp=128)[
                                :, :, fcb*128:(fcb+4)*128],
                            [], [W1B.k()], note="w1b")
                    for f4 in range(4):
                        fc = fcb + f4
                        for tg in range(2):
                            bk = env.bank()
                            mms = [(W1B.ap[:, i*512 + f4*128: i*512 + (f4+1)*128],
                                    fmi_rhs(OF, i, tg*4, (tg+1)*4)) for i in range(8)]
                            env.mm(bk.ap()[:, 0:512], mms, [W1B.k(), OF.k()],
                                   [(bk.name, "all")])
                            yo = (sub*4 + f4)*KB + tg*512
                            p.add("dve", lambda e, yo=yo, bk=bk, fc=fc: e.tensor_scalar(
                                      YB.ap[:, yo:yo+512], bk.ap()[:, 0:512],
                                      BF1C.ap[:, fc:fc+1], 0.0, ALU.add, ALU.max),
                                  [(bk.name, "all"), BF1C.k()], [YB.k(yo, yo+512)])
                env.dma("qpool", W2B.ap.rearrange("p (u f) -> p u f", u=8),
                        w["L0_attn_W2"][l].rearrange("(u pp) f -> pp u f", pp=128)[
                            :, b*8:(b+1)*8, :],
                        [], [W2B.k()], note="w2b")
                for j in range(8):
                    t = t0h + j
                    for h2 in range(2):
                        bk = env.bank()
                        mms = [(YB.ap[:, f8*KB + j*128: f8*KB + (j+1)*128],
                                W2B.ap[:, f8*KB + h2*512: f8*KB + (h2+1)*512])
                               for f8 in range(8)]
                        rd = [YB.k(), W2B.k()]
                        if b == 0:
                            mms.append((env.ones_bf.ap()[0:1, :],
                                        BRWF.ap[0:1, h2*512:(h2+1)*512]))
                            rd += [("ones_bf", "all"), BRWF.k()]
                        env.mm(bk.ap()[:, 0:512], mms, rd, [(bk.name, "all")])
                        so = t*D + h2*512
                        if b == 0:
                            p.add("dve", lambda e, so=so, h2=h2: e.tensor_mul(
                                      OUT.ap[:, so:so+512], OUT.ap[:, so:so+512],
                                      G1B.ap[:, h2*512:(h2+1)*512]),
                                  [OUT.k(so, so+512), G1B.k()], [OUT.k(so, so+512)])
                        p.add("dve", lambda e, so=so, bk=bk: e.tensor_add(
                                  OUT.ap[:, so:so+512], OUT.ap[:, so:so+512],
                                  bk.ap()[:, 0:512]),
                              [(bk.name, "all"), OUT.k(so, so+512)],
                              [OUT.k(so, so+512)])
        env.ln_stats(lambda t: OUT.ap[:, t*D:(t+1)*D], lambda t: [OUT.k(t*D, (t+1)*D)])
        env.ln_apply(lambda t: OUT.ap[:, t*D:(t+1)*D], lambda t: [OUT.k(t*D, (t+1)*D)],
                     lambda t: OUT.ap[:, t*D:(t+1)*D], lambda t: [OUT.k(t*D, (t+1)*D)])

    bcast_tile(G2B, w["L0_attn_g2"][hi-1], "g2f")
    bcast_tile(BVR, w["L0_attn_be2"][hi-1], "be2f")
    for t in range(NTILE):
        o = t * D
        p.add("dve", lambda e, o=o: e.tensor_mul(OUT.ap[:, o:o+D], OUT.ap[:, o:o+D],
                                                 G2B.ap[:, 0:KB]),
              [OUT.k(o, o+D), G2B.k()], [OUT.k(o, o+D)])
        p.add("dve", lambda e, o=o: e.tensor_add(OUT.ap[:, o:o+D], OUT.ap[:, o:o+D],
                                                 BVR.ap[:, 0:KB]),
              [OUT.k(o, o+D), BVR.k()], [OUT.k(o, o+D)])
    if last:
        env.ln_stats(lambda t: OUT.ap[:, t*D:(t+1)*D], lambda t: [OUT.k(t*D, (t+1)*D)])
        env.ln_apply_add(lambda t: OUT.ap[:, t*D:(t+1)*D],
                         lambda t: [OUT.k(t*D, (t+1)*D)])
    else:
        env.dma("qsp", aux_out.rearrange("(u pp) f -> pp u f", pp=128),
                OUT.ap.rearrange("p (u f) -> p u f", u=NTILE),
                [OUT.k()], [("AUXOUT", "all")], note="auxout")



SLOT_WEIGHTS = {
    "L0_attn": ["L0_attn_Wq", "L0_attn_bq", "L0_attn_Wk", "L0_attn_bk",
                "L0_attn_Wv", "L0_attn_bv", "L0_attn_Wo", "L0_attn_bo",
                "L0_attn_omega", "L0_attn_g1", "L0_attn_be1", "L0_attn_W1",
                "L0_attn_bf1", "L0_attn_W2", "L0_attn_bf2", "L0_attn_g2",
                "L0_attn_be2"],
    "L0_ffn": ["L0_ffn_W_r", "L0_ffn_W_k", "L0_ffn_W_v", "L0_ffn_mu_r", "L0_ffn_mu_k"],
    "L1_attn": ["L1_attn_W_aiv", "L1_attn_decay_bias", "L1_attn_W_mix", "L1_attn_b_mix"],
    "L1_ffn": ["L1_ffn_shared_up", "L1_ffn_shared_down", "L1_ffn_routed_up",
               "L1_ffn_routed_down", "L1_ffn_router_w"],
    "L2_attn": ["L2_attn_w_q", "L2_attn_w_k", "L2_attn_w_v", "L2_attn_w_o"],
    "L2_ffn": ["L2_ffn_weight", "L2_ffn_bias"],
    "L3_attn": ["L3_attn_Wq", "L3_attn_bq", "L3_attn_Wk", "L3_attn_bk",
                "L3_attn_Wv", "L3_attn_bv", "L3_attn_Wo", "L3_attn_bo"],
    "L3_ffn": ["L3_ffn_G1_R", "L3_ffn_G2_R", "L3_ffn_G1_I", "L3_ffn_G2_I",
               "L3_ffn_bias_real", "L3_ffn_bias_imag"],
}

INPUT_SHAPES = {
    "L0_attn_Wq": (6, 1024, 1024), "L0_attn_bq": (6, 1024),
    "L0_attn_Wk": (6, 1024, 1024), "L0_attn_bk": (6, 1024),
    "L0_attn_Wv": (6, 1024, 1024), "L0_attn_bv": (6, 1024),
    "L0_attn_Wo": (6, 1024, 1024), "L0_attn_bo": (6, 1024),
    "L0_attn_omega": (6, 64, 64), "L0_attn_g1": (6, 1024), "L0_attn_be1": (6, 1024),
    "L0_attn_W1": (6, 1024, 4096), "L0_attn_bf1": (6, 4096),
    "L0_attn_W2": (6, 4096, 1024), "L0_attn_bf2": (6, 1024),
    "L0_attn_g2": (6, 1024), "L0_attn_be2": (6, 1024),
    "L0_ffn_W_r": (1024, 1024), "L0_ffn_W_k": (1024, 1024), "L0_ffn_W_v": (1024, 1024),
    "L0_ffn_mu_r": (1024,), "L0_ffn_mu_k": (1024,),
    "L1_attn_W_aiv": (1024, 3072), "L1_attn_decay_bias": (1024,),
    "L1_attn_W_mix": (1024, 1024), "L1_attn_b_mix": (1024,),
    "L1_ffn_shared_up": (2, 512, 1024), "L1_ffn_shared_down": (2, 1024, 512),
    "L1_ffn_routed_up": (16, 512, 1024), "L1_ffn_routed_down": (16, 1024, 512),
    "L1_ffn_router_w": (16, 1024),
    "L2_attn_w_q": (16, 1024, 64), "L2_attn_w_k": (16, 1024, 64),
    "L2_attn_w_v": (16, 1024, 64), "L2_attn_w_o": (1024, 1024),
    "L2_ffn_weight": (1024, 1024), "L2_ffn_bias": (1024,),
    "L3_attn_Wq": (1024, 1024), "L3_attn_bq": (1024,),
    "L3_attn_Wk": (1024, 1024), "L3_attn_bk": (1024,),
    "L3_attn_Wv": (1024, 1024), "L3_attn_bv": (1024,),
    "L3_attn_Wo": (1024, 1024), "L3_attn_bo": (1024,),
    "L3_ffn_G1_R": (1, 32, 4, 32), "L3_ffn_G2_R": (4, 16, 1, 16),
    "L3_ffn_G1_I": (1, 32, 4, 32), "L3_ffn_G2_I": (4, 16, 1, 16),
    "L3_ffn_bias_real": (512,), "L3_ffn_bias_imag": (512,),
}

SLOT_FNS = {}


def build_program(slot_names):
    nc = bass.Bass("TRN2", target_bir_lowering=False, dynamic_dma_scratch_size=16384)
    xin = nc.dram_tensor("x_in", [T_TOK, D], F32, kind="ExternalInput").ap()
    xout = nc.dram_tensor("x_out", [T_TOK, D], F32, kind="ExternalOutput").ap()
    specs = []
    wnames = []
    for s in slot_names:
        if "@" in s:
            base, rng = s.split("@")
            lo, hi = map(int, rng.split("-"))
            specs.append((base, lo, hi))
        else:
            base = s
            specs.append((s, None, None))
        for name in SLOT_WEIGHTS[base]:
            if name not in wnames:
                wnames.append(name)
    w = {}
    for name in wnames:
        w[name] = nc.dram_tensor(name, list(INPUT_SHAPES[name]), F32,
                                 kind="ExternalInput").ap()
    env = Env(nc)
    env.init_consts()
    env.dma("qsp", env.x.ap().rearrange("p (u f) -> p u f", u=NTILE),
            xin.rearrange("(u p) f -> p u f", p=128),
            [], [("x_tm", t) for t in range(NTILE)], note="xin")
    names = wnames + ["x_in"]
    has_aux_out = False
    for (s, slo, shi) in specs:
        if slo is None:
            SLOT_FNS[s](env, w)
        else:
            fst, lst = (slo == 0), (shi == 6)
            ai = ao = None
            if not fst:
                ai = nc.dram_tensor("aux_in", [T_TOK, D], F32,
                                    kind="ExternalInput").ap()
                names.append("aux_in")
            if not lst:
                ao = nc.dram_tensor("aux_out", [T_TOK, D], F32,
                                    kind="ExternalOutput").ap()
                has_aux_out = True
            SLOT_FNS[s](env, w, lo=slo, hi=shi, first=fst, last=lst,
                        aux_in=ai, aux_out=ao)
    env.dma("qsp", xout.rearrange("(u p) f -> p u f", p=128),
            env.x.ap().rearrange("p (u f) -> p u f", u=NTILE),
            [("x_tm", t) for t in range(NTILE)], [("OUT", "all")], note="xout")
    fin_reads = [("OUT", "all"), ("TAPS", "all")]
    if has_aux_out:
        fin_reads.append(("AUXOUT", "all"))
    env.prog.add("sp", lambda e: e.nop(), fin_reads, [], note="fin")
    env.prog.render()
    return nc, names


SLOT_ORDER = ["L0_attn", "L0_ffn", "L1_attn", "L1_ffn",
              "L2_attn", "L2_ffn", "L3_attn", "L3_ffn"]

GROUPS = [SLOT_ORDER]


def kernel(**inputs):
    x = np.ascontiguousarray(inputs["L0_attn_x"], dtype=np.float32)
    groups = GROUPS
    out = x.copy()
    aux = [None] * 8
    for g in groups:
        nc, names = build_program(g)
        in_maps = []
        for b in range(8):
            m = {}
            for n in names:
                if n == "x_in":
                    m[n] = out[b]
                elif n == "aux_in":
                    m[n] = aux[b]
                else:
                    m[n] = np.ascontiguousarray(inputs[n], dtype=np.float32)
            in_maps.append(m)
        res = run_bass_kernel_spmd(nc, in_maps, core_ids=list(range(8)))
        out = np.stack([res.results[b]["x_out"] for b in range(8)], axis=0)
        if "aux_out" in res.results[0]:
            aux = [np.asarray(res.results[b]["aux_out"]) for b in range(8)]
    return out


SLOT_FNS["L0_attn"] = slot_L0_attn
SLOT_FNS["L0_ffn"] = slot_L0_ffn
SLOT_FNS["L1_attn"] = slot_L1_attn
SLOT_FNS["L2_ffn"] = slot_L2_ffn
SLOT_FNS["L3_ffn"] = slot_L3_ffn
SLOT_FNS["L2_attn"] = slot_L2_attn
SLOT_FNS["L3_attn"] = slot_L3_attn
SLOT_FNS["L1_ffn"] = slot_L1_ffn
